# version 03
# baseline (speedup 1.0000x reference)
import numpy as np
from contextlib import ExitStack
import concourse.bass as bass
import concourse.mybir as mybir
from concourse.bass_utils import run_bass_kernel_spmd

F32 = mybir.dt.float32
BF16 = mybir.dt.bfloat16
U32 = mybir.dt.uint32
AF = mybir.ActivationFunctionType
OP = mybir.AluOpType
AX = mybir.AxisListType

T, TO, D, KD, R, I, MI = 4096, 2048, 2048, 16, 64, 5632, 44
NST = 16
NH = 8
EPS = 1e-5
TS = 512
NTS = T // TS
KQ = 2
NQ = KD // KQ
NG = 2
NGR = NH // NG
TAU = 512
NTAU = TO // TAU
WB = 2
NB = MI // WB
RG = [[0, 1], [2, 3], [4, 5], [6, 7]]


class Plan:
    def __init__(self):
        self.ops = []
        self.cnt = {}
        self.pend = {}

    def c(self, sem):
        return self.cnt.get(sem, 0)

    def add(self, eng, fn, waits=(), inc=None, amt=1, tag=""):
        w = list(self.pend.pop(eng, []))
        for s, v in waits:
            assert v <= self.cnt.get(s, 0), f"wait-ahead {s}:{v}>{self.cnt.get(s,0)} @{tag}"
            if v > 0:
                w.append((s, v))
        incs = []
        if inc is not None:
            self.cnt[inc] = self.cnt.get(inc, 0) + amt
            incs.append((inc, amt))
        self.ops.append([eng, fn, tuple(w), incs, tag])
        return self.cnt.get(inc, 0) if inc is not None else None

    def drain(self, eng, tag=""):
        self.add(eng, lambda e: e.drain(), tag=f"drain-{tag}")

    def wait_next(self, eng, sem, val):
        self.pend.setdefault(eng, []).append((sem, val))


def build(dbg=False, stages=9):
    nc = bass.Bass("TRN2", target_bir_lowering=False)

    x_sc = nc.dram_tensor("x_sc", [T, D], F32, kind="ExternalInput").ap()
    x_own = nc.dram_tensor("x_own", [D, TO], F32, kind="ExternalInput").ap()
    wxp_t = nc.dram_tensor("wxp_t", [D, 80], F32, kind="ExternalInput").ap()
    wdt_t = nc.dram_tensor("wdt_t", [R, D], F32, kind="ExternalInput").ap()
    wg_t = nc.dram_tensor("wg_t", [D, I], F32, kind="ExternalInput").ap()
    wu_t = nc.dram_tensor("wu_t", [D, I], F32, kind="ExternalInput").ap()
    wd_t = nc.dram_tensor("wd_t", [I, D], F32, kind="ExternalInput").ap()
    alog_r = nc.dram_tensor("alog_r", [128, KD * NH], F32, kind="ExternalInput").ap()
    d_r = nc.dram_tensor("d_r", [128, KD], F32, kind="ExternalInput").ap()
    bdt_r = nc.dram_tensor("bdt_r", [128, KD], F32, kind="ExternalInput").ap()
    gs_r = nc.dram_tensor("gs_r", [128, MI], F32, kind="ExternalInput").ap()
    us_r = nc.dram_tensor("us_r", [128, MI], F32, kind="ExternalInput").ap()
    ds_r = nc.dram_tensor("ds_r", [128, KD], F32, kind="ExternalInput").ap()
    ident = nc.dram_tensor("ident", [128, 128], F32, kind="ExternalInput").ap()
    consts_r = nc.dram_tensor("consts_r", [128, 8], F32, kind="ExternalInput").ap()
    off_r = nc.dram_tensor("off_r", [1, 1], U32, kind="ExternalInput").ap()
    off2_r = nc.dram_tensor("off2_r", [1, 1], U32, kind="ExternalInput").ap()
    out_ft = nc.dram_tensor("out_ft", [D, TO], F32, kind="ExternalOutput").ap()

    xn32_d = nc.dram_tensor("xn32_d", [D, T], F32).ap()
    xnbf_d = nc.dram_tensor("xnbf_d", [D, T], BF16).ap()
    dt_d = nc.dram_tensor("dt_d", [D, T], F32).ap()
    xpjbf_d = nc.dram_tensor("xpjbf_d", [2 * NH, T], BF16).ap()
    yin_q = [nc.dram_tensor(f"yin{q}", [2, D, TS], F32) for q in range(4)]
    yout_q = [nc.dram_tensor(f"yout{q}", [D, TS], F32) for q in range(4)]
    g_d = nc.dram_tensor("g_d", [D, TO], F32).ap()
    xnown_d = nc.dram_tensor("xnown_d", [D, 12 * 512], F32).ap()

    p = Plan()

    rr = lambda ap_: ap_.rearrange("(k p) t -> p k t", p=128)
    xn32_v, xnbf_v, dt_v, g_v = rr(xn32_d), rr(xnbf_d), rr(dt_d), rr(g_d)
    xnown_v = rr(xnown_d)
    out_v, xown_v = rr(out_ft), rr(x_own)
    yin_v = [y.ap().rearrange("b (k p) t -> p b k t", p=128) for y in yin_q]
    yout_v = [y.ap().rearrange("(k p) t -> p k t", p=128) for y in yout_q]
    wg_v = wg_t.rearrange("(k p) i -> p k i", p=128)
    wu_v = wu_t.rearrange("(k p) i -> p k i", p=128)
    wd_v = wd_t.rearrange("(q p) h -> p q h", p=128)
    wxp_v = wxp_t.rearrange("(k p) e -> p k e", p=128)

    perst = ExitStack()
    SB = lambda n, s, d=F32: perst.enter_context(nc.sbuf_tensor(n, s, d))
    a_sb = SB("a_sb", [128, KD * NH])
    d_sb = SB("d_sb", [128, KD])
    bdt_sb = SB("bdt_sb", [128, KD])
    gs_sb = SB("gs_sb", [128, MI])
    ngs_sb = SB("ngs_sb", [128, MI])
    us_sb = SB("us_sb", [128, MI])
    ds_sb = SB("ds_sb", [128, KD])
    nds_sb = SB("nds_sb", [128, KD])
    hcar_sb = SB("hcar_sb", [128, KD * NH])
    off_sb = SB("off_sb", [1, 1], U32)
    off2_sb = SB("off2_sb", [1, 1], U32)
    cst_sb = SB("cst_sb", [128, 8])
    for _i, _cv in enumerate([EPS, 1.0, -0.5, -1.0, 1.0 / D, 30.0]):
        nc.const_aps.register(F32, _cv, cst_sb[:, _i:_i + 1])

    psA = [perst.enter_context(nc.psum_tensor(f"psA{i}", [128, 512], F32)) for i in range(2)]
    psB = [perst.enter_context(nc.psum_tensor(f"psB{i}", [128, 512], F32)) for i in range(2)]
    psD = [perst.enter_context(nc.psum_tensor(f"psD{i}", [128, 512], F32)) for i in range(2)]

    HOLD = {}

    phA = ExitStack()
    SA = lambda n, s, d=F32: phA.enter_context(nc.sbuf_tensor(n, s, d))
    idf_sb = SA("idf_sb", [128, 128])
    wxp_sb = SA("wxp_sb", [128, KD * 80], BF16)
    wdt_sb = SA("wdt_sb", [R, D], BF16)
    wxpf_sb = SA("wxpf_sb", [128, KD * 80])
    wdtf_sb = SA("wdtf_sb", [R, D])
    xtm = [SA(f"xtm{i}", [128, D]) for i in range(2)]
    x2_sb = SA("x2_sb", [128, D], BF16)
    sst = SA("sst", [128, 8])
    xntm_sb = SA("xntm_sb", [128, D])
    xnft_sb = SA("xnft_sb", [128, KD * 512])
    xnbf_sb = SA("xnbf_sb", [128, KD * 512], BF16)
    xpj_sb = SA("xpj_sb", [80, T])
    dtp_sb = SA("dtp_sb", [R, T], BF16)
    dtq_sb = SA("dtq_sb", [128, 4 * 512])
    spt_sb = SA("spt_sb", [128, 512])
    bcst_sb = SA("bcst_sb", [2 * NH, T], BF16)

    for name, dst, src in [
        ("alog", a_sb[:, :], alog_r), ("d", d_sb[:, :], d_r), ("bdt", bdt_sb[:, :], bdt_r),
        ("gs", gs_sb[:, :], gs_r), ("us", us_sb[:, :], us_r), ("ds", ds_sb[:, :], ds_r),
        ("idf", idf_sb[:, :], ident), ("csts", cst_sb[:, :], consts_r),
        ("off", off_sb[:, :], off_r), ("off2", off2_sb[:, :], off2_r),
        ("wxpf", wxpf_sb.rearrange("p (k e) -> p k e", e=80), wxp_v),
        ("wdtf", wdtf_sb[:, :], wdt_t),
    ]:
        p.add("sy", (lambda d_, s_: lambda e: e.dma_start(out=d_, in_=s_))(dst, src),
              inc="qPRE", amt=16, tag=f"pre-{name}")
    preLD = p.c("qPRE")

    p.add("sc", lambda e: e.activation(a_sb[:, :], a_sb[:, :], AF.Exp),
          waits=[("qPRE", preLD)], inc="sS", tag="pre-expA")
    p.add("ve", lambda e: e.tensor_scalar_mul(a_sb[:, :], a_sb[:, :], -1.0),
          waits=[("sS", p.c("sS"))], tag="pre-negA")
    p.add("ve", lambda e: e.tensor_scalar_mul(ngs_sb[:, :], gs_sb[:, :], -1.0),
          waits=[("qPRE", preLD)], tag="pre-neggs")
    p.add("ve", lambda e: e.tensor_scalar_mul(nds_sb[:, :], ds_sb[:, :], -1.0), tag="pre-negds")
    p.add("ve", lambda e: e.tensor_copy(wxp_sb[:, :], wxpf_sb[:, :]), tag="pre-cwxp")
    p.add("ve", lambda e: e.tensor_copy(wdt_sb[:, :], wdtf_sb[:, :]), tag="pre-cwdt")
    p.add("ve", lambda e: e.memset(hcar_sb[:, :], 0.0), inc="sV", tag="pre-hcar")
    preV = p.c("sV")

    SUM, SSQ, M_, MSQ, VAR, LV, RSTD = range(7)
    sv_ = lambda i: sst[:, i:i + 1]
    ISQD = float(1.0 / np.sqrt(np.float64(D)))

    for sc_i in range(8):
        for c4 in range(4):
            ci = sc_i * 4 + c4
            buf = xtm[ci % 2]
            xl = p.add("sy", (lambda b_, r0: lambda e: e.dma_start(
                out=b_[:, :], in_=x_sc[r0:r0 + 128, :]))(buf, ci * 128),
                waits=[("sV", p.c("sV")), ("sS", p.c("sS"))] if ci >= 2 else [],
                inc=f"qXC{ci % 2}", amt=16, tag=f"A-x-{ci}")
            p.add("ve", (lambda b_: lambda e: e.reduce_sum(sv_(SUM), b_[:, :], AX.X))(buf),
                  waits=[(f"qXC{ci % 2}", xl), ("sG", p.c("sG")), ("sS", p.c("sS"))],
                  inc="sV", tag=f"A-sum-{ci}")
            sum_d = p.c("sV")
            p.add("sc", (lambda b_: lambda e: e.activation(
                x2_sb[:, :], b_[:, :], AF.Square, scale=ISQD))(buf),
                waits=[(f"qXC{ci % 2}", xl), ("sV", p.c("sV"))], inc="sS", tag=f"A-sq-{ci}")
            sq_d = p.c("sS")
            p.add("ve", lambda e: e.reduce_sum(sv_(SSQ), x2_sb[:, :], AX.X),
                  waits=[("sS", sq_d), ("sG", p.c("sG"))], inc="sV", tag=f"A-ssq-{ci}")
            ssq_d = p.c("sV")
            p.add("sc", lambda e: e.activation(sv_(M_), sv_(SUM), AF.Copy, scale=1.0 / D),
                  waits=[("sV", sum_d)], tag=f"A-m-{ci}")
            p.add("sc", lambda e: e.activation(sv_(MSQ), sv_(SUM), AF.Square, scale=1.0 / D),
                  inc="sS", tag=f"A-msq-{ci}")
            msq_d = p.c("sS")
            p.add("gp", lambda e: e.tensor_sub(sv_(VAR), sv_(SSQ), sv_(MSQ)),
                  waits=[("sV", ssq_d), ("sS", msq_d)], inc="sG", tag=f"A-var-{ci}")
            var_d = p.c("sG")
            p.add("sc", lambda e: e.activation(sv_(LV), sv_(VAR), AF.Ln, bias=EPS),
                  waits=[("sG", var_d)], tag=f"A-lv-{ci}")
            p.drain("sc", f"Alv-{ci}")
            p.add("sc", lambda e: e.activation(sv_(RSTD), sv_(LV), AF.Exp, scale=-0.5),
                  inc="sS", tag=f"A-rstd-{ci}")
            rstd_d = p.c("sS")
            p.add("ve", (lambda b_: lambda e: e.tensor_scalar(
                xntm_sb[:, :], b_[:, :], sv_(M_), sv_(RSTD), OP.subtract, OP.mult))(buf),
                waits=[("sS", rstd_d), ("sT", p.c("sT"))], inc="sV", tag=f"A-nrm-{ci}")
            for kg in range(4):
                for j in range(4):
                    k = kg * 4 + j
                    p.add("te", (lambda k_, j_: lambda e: e.transpose(
                        psA[0][:, j_ * 128:(j_ + 1) * 128],
                        xntm_sb[:, k_ * 128:(k_ + 1) * 128], idf_sb[:, :]))(k, j),
                        waits=[("sV", p.c("sV"))] if j == 0 else [],
                        inc="sT" if j == 3 else None, tag=f"A-tr-{ci}-{k}")
                tr_d = p.c("sT")
                dst = xnft_sb.rearrange("p (k c) -> p k c", c=512)[
                    :, kg * 4:(kg + 1) * 4, c4 * 128:(c4 + 1) * 128]
                p.add("ve", (lambda d_: lambda e: e.tensor_copy(
                    d_, psA[0].rearrange("p (k c) -> p k c", c=128)))(dst),
                    waits=[("sT", tr_d)] + ([("qSPA", p.c("qSPA")), ("sS", p.c("sS")),
                                             ("qSPO", p.c("qSPO"))]
                                            if kg == 0 and c4 == 0 else []),
                    inc="sV", tag=f"A-ev-{ci}-{kg}")
        ev_all = p.c("sV")
        p.add("sc", lambda e: e.activation(xnbf_sb[:, :], xnft_sb[:, :], AF.Copy),
              waits=[("sV", ev_all), ("qSPA", p.c("qSPA")), ("sT", p.c("sT"))],
              inc="sS", tag=f"A-cast-{sc_i}")
        cast_d = p.c("sS")
        p.add("gp", (lambda s_: lambda e: e.dma_start(
            out=xn32_v[:, :, s_ * 512:(s_ + 1) * 512],
            in_=xnft_sb.rearrange("p (k c) -> p k c", c=512)))(sc_i),
            waits=[("sV", ev_all)], inc="qSPA", amt=16, tag=f"A-sp32-{sc_i}")
        p.add("gp", (lambda s_: lambda e: e.dma_start(
            out=xnbf_v[:, :, s_ * 512:(s_ + 1) * 512],
            in_=xnbf_sb.rearrange("p (k c) -> p k c", c=512)))(sc_i),
            waits=[("sS", cast_d)], inc="qSPA", amt=16, tag=f"A-spbf-{sc_i}")

        def _ownspill(s_):
            def fn(e):
                sv = e.value_load(off2_sb[0:1, 0:1])
                return e.dma_start(
                    out=xnown_v[:, :, bass.ts(sv + s_, 512)],
                    in_=xnft_sb.rearrange("p (k c) -> p k c", c=512))
            return fn
        p.add("sy", _ownspill(sc_i), waits=[("sV", ev_all), ("qPRE", preLD)],
              inc="qSPO", amt=16, tag=f"A-spown-{sc_i}")
        for k in range(KD):
            p.add("te", (lambda k_: lambda e: e.matmul(
                psA[1][0:80, :], wxp_sb[:, k_ * 80:(k_ + 1) * 80],
                xnbf_sb[:, k_ * 512:(k_ + 1) * 512],
                start=(k_ == 0), stop=(k_ == KD - 1)))(k),
                waits=[("sS", cast_d), ("sV", preV)] if k == 0 else [],
                inc="sT" if k == KD - 1 else None, tag=f"A-xp-{sc_i}-{k}")
        xp_d = p.c("sT")
        p.add("sc", (lambda s_: lambda e: e.copy(
            xpj_sb[:, s_ * 512:(s_ + 1) * 512], psA[1][0:80, :]))(sc_i),
            waits=[("sT", xp_d)], tag=f"A-xpev-{sc_i}")
        p.drain("sc", f"Axpev-{sc_i}")
        p.add("sc", (lambda s_: lambda e: e.activation(
            dtp_sb[:, s_ * 512:(s_ + 1) * 512], xpj_sb[0:R, s_ * 512:(s_ + 1) * 512],
            AF.Copy))(sc_i), inc="sS", tag=f"A-dtp-{sc_i}")
        dtp_d = p.c("sS")
        for k in range(KD):
            par = k % 2
            p.add("te", (lambda s_, k_, par_: lambda e: e.matmul(
                psD[par_][:, :], wdt_sb[:, k_ * 128:(k_ + 1) * 128],
                dtp_sb[:, s_ * 512:(s_ + 1) * 512], start=True, stop=True))(sc_i, k, par),
                waits=[("sS", dtp_d if k == 0 else p.c("sS"))],
                inc="sT", tag=f"A-dtm-{sc_i}-{k}")
            dtm_d = p.c("sT")
            p.add("sc", (lambda par_, k_: lambda e: e.activation(
                spt_sb[:, :], psD[par_][:, :], AF.Exp, bias=bdt_sb[:, k_:k_ + 1]))(par, k),
                waits=[("sT", dtm_d)] + ([("qDTS", p.c("qDTS"))] if k % 4 == 0 else []),
                tag=f"A-spe-{sc_i}-{k}")
            p.drain("sc", f"Asp-{sc_i}-{k}")
            p.add("sc", (lambda k_: lambda e: e.activation(
                dtq_sb[:, (k_ % 4) * 512:(k_ % 4 + 1) * 512], spt_sb[:, :],
                AF.Ln, bias=1.0))(k), inc="sS", tag=f"A-spl-{sc_i}-{k}")
            if k % 4 == 3:
                p.add("gp", (lambda s_, kq_: lambda e: e.dma_start(
                    out=dt_v[:, kq_ * 4:(kq_ + 1) * 4, s_ * 512:(s_ + 1) * 512],
                    in_=dtq_sb.rearrange("p (k c) -> p k c", c=512)))(sc_i, k // 4),
                    waits=[("sS", p.c("sS"))], inc="qDTS", amt=16, tag=f"A-dtsp-{sc_i}-{k}")
    p.add("sc", lambda e: e.activation(bcst_sb[:, :], xpj_sb[R:R + 2 * NH, :], AF.Copy),
          inc="sS", tag="A-bccast")
    spbc = p.add("gp", lambda e: e.dma_start(out=xpjbf_d, in_=bcst_sb[:, :]),
                 waits=[("sS", p.c("sS"))], inc="qSPA", amt=16, tag="A-spbc")
    phA_SP = p.c("qSPA")

    barw = [("sV", p.c("sV")), ("sS", p.c("sS")), ("sT", p.c("sT")), ("sG", p.c("sG")),
            ("qSPA", p.c("qSPA")), ("qDTS", p.c("qDTS")), ("qPRE", p.c("qPRE")),
            ("qSPO", p.c("qSPO")), ("qXC0", p.c("qXC0")), ("qXC1", p.c("qXC1"))]
    for eng in ("ve", "sc", "te", "gp", "sy"):
        for s, v in barw:
            p.wait_next(eng, s, v)
    phA.close()

    mn = ExitStack()
    SM = lambda n, s, d=F32: mn.enter_context(nc.sbuf_tensor(n, s, d))
    dtk_sb = SM("dtk_sb", [128, KQ * TS])
    xnk_sb = SM("xnk_sb", [128, KQ * TS])
    ab_sb = SM("ab_sb", [128, 2 * NG * TS])
    ub_sb = SM("ub_sb", [128, 2 * NG * TS])
    hb_sb = SM("hb_sb", [128, 2 * NG * TS])
    pr_sb = SM("pr_sb", [128, 2 * TS * NG])
    yk_sb = SM("yk_sb", [128, KQ * TS])
    yk2_sb = SM("yk2_sb", [128, KQ * TS])
    bbc_sb = SM("bbc_sb", [128, NGR * NG * TS], BF16)
    cbc_sb = SM("cbc_sb", [128, NGR * NG * TS], BF16)
    stg_sb = SM("stg_sb", [128, KD * WB * 128])
    wgb_sb = SM("wgb_sb", [128, WB * KD * 128], BF16)
    wub_sb = SM("wub_sb", [128, WB * KD * 128], BF16)
    stgd_sb = SM("stgd_sb", [128, 4 * 256])
    wdb_sb = SM("wdb_sb", [128, 4 * 256], BF16)
    hdn_sb = SM("hdn_sb", [128, MI * TAU], BF16)
    xnm_sb = SM("xnm_sb", [128, KD * TAU], BF16)
    fe1_sb = SM("fe1_sb", [128, 512])
    fe2_sb = SM("fe2_sb", [128, 512])
    ft1_sb = SM("ft1_sb", [128, 2 * 512])
    ft2_sb = SM("ft2_sb", [128, 2 * 512])
    dcl_sb = SM("dcl_sb", [128, 512])
    dg_sb = SM("dg_sb", [128, 2 * 512])
    fg_sb = SM("fg_sb", [128, 2 * 512])
    fy_sb = SM("fy_sb", [128, 2 * 512])
    fxn_sb = SM("fxn_sb", [128, 2 * 512])
    fx_sb = SM("fx_sb", [128, 2 * 512])

    st = dict(gcnt=0, dlv=0, zlast=0,
              ttspar=[0, 0], prodpar=[0, 0], carpar=[0, 0], redpar=[0, 0],
              ubc=[0] * NGR, pbc=[0] * NGR, bc={},
              yk_pair=None, yk2_pair=None, ysp2=[None] * 4,
              t1g=[0, 0], t2g=[0, 0], hdng=[0, 0], dgg=[0, 0], dglast=0,
              t1last=0, culast=0, cdlast=0, gulast=0, dnlast=0, dsglast=0,
              finlast=0, gsp_pair=None, g_sp=[None] * NTAU)

    bbc_v = bbc_sb.rearrange("p (q n c) -> p q n c", q=NGR, c=TS)
    cbc_v = cbc_sb.rearrange("p (q n c) -> p q n c", q=NGR, c=TS)
    abv = ab_sb.rearrange("p (q n c) -> p q n c", q=2, c=TS)
    ubv = ub_sb.rearrange("p (q n c) -> p q n c", q=2, c=TS)
    hbv = hb_sb.rearrange("p (q n c) -> p q n c", q=2, c=TS)
    prv = pr_sb.rearrange("p (q c n) -> p q c n", q=2, n=NG)

    def emit_bcast(ts, gi):
        c0 = ts * TS
        nb = gi * NG
        p.add("sc", (lambda nb_, gi_, c0_: lambda e: e.dma_start(
            out=bbc_v[:, gi_, :, :],
            in_=xpjbf_d[nb_:nb_ + NG, c0_:c0_ + TS].unsqueeze(0).broadcast_to(
                [128, NG, TS])))(nb, gi, c0),
            waits=[("sG", st["ubc"][gi]), ("qSPA", spbc)], inc=f"qBC{gi}", amt=16,
            tag=f"s-bb-{ts}-{gi}")
        cb = p.add("sc", (lambda nb_, gi_, c0_: lambda e: e.dma_start(
            out=cbc_v[:, gi_, :, :],
            in_=xpjbf_d[NH + nb_:NH + nb_ + NG, c0_:c0_ + TS].unsqueeze(0).broadcast_to(
                [128, NG, TS])))(nb, gi, c0),
            waits=[("sG", st["pbc"][gi])], inc=f"qBC{gi}", amt=16, tag=f"s-cb-{ts}-{gi}")
        st["bc"][gi] = cb

    def gen_scan():
      for ts in range(NTS):
        c0 = ts * TS
        q4 = ts % 4
        hh = ts // 4
        for _gi in range(NGR):
            emit_bcast(ts, _gi)
        for kq in range(NQ):
            xl = p.add("sy", (lambda kq_, c0_: lambda e: e.dma_start(
                out=xnk_sb.rearrange("p (k c) -> p k c", c=TS),
                in_=xn32_v[:, kq_ * KQ:(kq_ + 1) * KQ, c0_:c0_ + TS]))(kq, c0),
                waits=[("qSPA", phA_SP), ("sG", p.c("sG"))],
                inc="qXL", amt=16, tag=f"s-xl-{ts}-{kq}")
            dl = p.add("sy", (lambda kq_, c0_: lambda e: e.dma_start(
                out=dtk_sb.rearrange("p (k c) -> p k c", c=TS),
                in_=dt_v[:, kq_ * KQ:(kq_ + 1) * KQ, c0_:c0_ + TS]))(kq, c0),
                waits=[("sS", p.c("sS")), ("sG", st["zlast"])],
                inc="qDL", amt=16, tag=f"s-dl-{ts}-{kq}")
            st["dlv"] = dl
            for ki in range(KQ):
                p.add("gp", (lambda ki_: lambda e: e.tensor_mul(
                    xnk_sb[:, ki_ * TS:(ki_ + 1) * TS], dtk_sb[:, ki_ * TS:(ki_ + 1) * TS],
                    xnk_sb[:, ki_ * TS:(ki_ + 1) * TS]))(ki),
                    waits=[("qDL", dl), ("qXL", xl)] if ki == 0 else [],
                    inc="sG" if ki == KQ - 1 else None, tag=f"s-z-{ts}-{kq}-{ki}")
            p.drain("gp", f"z-{ts}-{kq}")
            st["zlast"] = p.c("sG")
            for ki in range(KQ):
                k = kq * KQ + ki
                for gi in range(NGR):
                    nb = gi * NG
                    cb_v = st["bc"][gi]
                    par = st["gcnt"] % 2
                    st["gcnt"] += 1
                    for n in range(NG):
                        p.add("sc", (lambda ki_, n_, k_, nb_, par_: lambda e: e.activation(
                            abv[:, par_, n_, :], dtk_sb[:, ki_ * TS:(ki_ + 1) * TS],
                            AF.Exp,
                            scale=a_sb[:, k_ * NH + nb_ + n_:k_ * NH + nb_ + n_ + 1]))(
                                ki, n, k, nb, par),
                            waits=[("sV", st["ttspar"][par]), ("qDL", st["dlv"])]
                            if n == 0 else [],
                            inc="sS" if n == NG - 1 else None, tag=f"s-e-{ts}-{k}-{gi}-{n}")
                    exp_d = p.c("sS")
                    for n in range(NG):
                        p.add("gp", (lambda ki_, n_, par_, gi_: lambda e: e.tensor_mul(
                            ubv[:, par_, n_, :], xnk_sb[:, ki_ * TS:(ki_ + 1) * TS],
                            bbc_v[:, gi_, n_, :]))(ki, n, par, gi),
                            waits=[(f"qBC{gi}", cb_v), ("sV", st["ttspar"][par])]
                            if n == 0 else [],
                            inc="sG" if n == NG - 1 else None, tag=f"s-u-{ts}-{k}-{gi}-{n}")
                    u_d = p.c("sG")
                    st["ubc"][gi] = u_d
                    for n in range(NG):
                        col = k * NH + nb + n
                        p.add("ve", (lambda n_, col_, par_: lambda e: e.tensor_tensor_scan(
                            hbv[:, par_, n_, :], abv[:, par_, n_, :], ubv[:, par_, n_, :],
                            hcar_sb[:, col_:col_ + 1], OP.mult, OP.add))(n, col, par),
                            waits=[("sS", exp_d), ("sG", u_d),
                                   ("sG", st["prodpar"][par]), ("sS", st["carpar"][par])]
                            if n == 0 else [],
                            inc="sV" if n == NG - 1 else None, tag=f"s-h-{ts}-{k}-{gi}-{n}")
                    tts_d = p.c("sV")
                    st["ttspar"][par] = tts_d
                    p.add("sc", (lambda k_, nb_, par_: lambda e: e.copy(
                        hcar_sb[:, k_ * NH + nb_:k_ * NH + nb_ + NG],
                        hbv[:, par_, :, TS - 1:TS]))(k, nb, par),
                        waits=[("sV", tts_d)], inc="sS", tag=f"s-c-{ts}-{k}-{gi}")
                    st["carpar"][par] = p.c("sS")
                    for n in range(NG):
                        p.add("gp", (lambda n_, par_, gi_: lambda e: e.tensor_mul(
                            prv[:, par_, :, n_:n_ + 1],
                            hbv[:, par_, n_, :].unsqueeze(2),
                            cbc_v[:, gi_, n_, :].unsqueeze(2)))(n, par, gi),
                            waits=[("sV", tts_d), ("sV", st["redpar"][par])]
                            if n == 0 else [],
                            inc="sG" if n == NG - 1 else None, tag=f"s-p-{ts}-{k}-{gi}-{n}")
                    prod_d = p.c("sG")
                    st["prodpar"][par] = prod_d
                    st["pbc"][gi] = prod_d
                    if gi % 2 == 1:
                        ykb = yk_sb if gi == 1 else yk2_sb
                        ypair = st["yk_pair"] if gi == 1 else st["yk2_pair"]
                        pr4 = pr_sb.rearrange("p (q c n) -> p c q n", q=2, n=NG)
                        p.add("ve", (lambda ki_, yb_: lambda e: e.tensor_reduce(
                            yb_[:, ki_ * TS:(ki_ + 1) * TS].unsqueeze(2).unsqueeze(3),
                            pr4, AX.XY, OP.add))(ki, ykb),
                            waits=[("sG", prod_d)] + ([ypair] if ypair and ki == 0 else []),
                            inc="sV", tag=f"s-r-{ts}-{k}-{gi}")
                        st["redpar"][0] = st["redpar"][1] = p.c("sV")
            ysem = f"qY{q4}"
            allred = p.c("sV")
            y1 = p.add("gp", (lambda kq_, q_, hh_: lambda e: e.dma_start(
                out=yin_v[q_][:, hh_, kq_ * KQ:(kq_ + 1) * KQ, :],
                in_=yk_sb.rearrange("p (k c) -> p k c", c=TS)))(kq, q4, hh),
                waits=[("sV", allred)], inc=ysem, amt=16, tag=f"s-y1-{ts}-{kq}")
            st["yk_pair"] = (ysem, y1)
            y2 = p.add("gp", (lambda kq_, q_, hh_: lambda e: e.dma_start(
                out=yin_v[q_][:, hh_, kq_ * KQ:(kq_ + 1) * KQ, :],
                in_=yk2_sb.rearrange("p (k c) -> p k c", c=TS),
                accum_op=OP.add))(kq, q4, hh),
                waits=[(ysem, y1)], inc=ysem, amt=16, tag=f"s-y2-{ts}-{kq}")
            st["yk2_pair"] = (ysem, y2)
            yield ts
        if hh == 1:
            st["ysp2"][q4] = st["yk2_pair"]

    def emit_rs(q):
        ysem, yv = st["ysp2"][q]
        p.add("gp", (lambda q_: lambda e: e.collective_compute(
            "ReduceScatter", OP.add, replica_groups=RG,
            ins=[yin_q[q_].ap()], outs=[yout_q[q_].ap()]))(q),
            waits=[(ysem, yv)], inc="qRS", tag=f"rs-{q}")

    def gen_ffn():
      for tau in range(NTAU):
        def _xmfn(t_):
            def fn(e):
                sv = e.value_load(off_sb[0:1, 0:1])
                return e.dma_start(
                    out=xnm_sb.rearrange("p (k c) -> p k c", c=TAU),
                    in_=xnbf_v[:, :, bass.ts(sv + t_, TAU)])
            return fn
        xm = p.add("sy", _xmfn(tau),
                   waits=[("qSPA", phA_SP), ("sT", st["gulast"]), ("qPRE", preLD)],
                   inc="qXM", amt=16, tag=f"f-xm-{tau}")
        for b in range(NB):
            lg = p.add("sy", (lambda b_: lambda e: e.dma_start(
                out=stg_sb.rearrange("p (k w c) -> p k (w c)", k=KD, w=WB),
                in_=wg_v[:, :, b_ * WB * 128:(b_ + 1) * WB * 128]))(b),
                waits=[("sV", st["culast"])], inc="qWGU", amt=16, tag=f"f-lg-{tau}-{b}")
            cg = p.add("ve", lambda e: e.tensor_copy(
                wgb_sb.rearrange("p (w k c) -> p w k c", w=WB, c=128),
                stg_sb.rearrange("p (k w c) -> p w k c", w=WB, c=128)),
                waits=[("qWGU", lg), ("sT", p.c("sT"))], inc="sV", tag=f"f-cg-{tau}-{b}")
            lu = p.add("sy", (lambda b_: lambda e: e.dma_start(
                out=stg_sb.rearrange("p (k w c) -> p k (w c)", k=KD, w=WB),
                in_=wu_v[:, :, b_ * WB * 128:(b_ + 1) * WB * 128]))(b),
                waits=[("sV", cg)], inc="qWGU", amt=16, tag=f"f-lu-{tau}-{b}")
            cu = p.add("ve", lambda e: e.tensor_copy(
                wub_sb.rearrange("p (w k c) -> p w k c", w=WB, c=128),
                stg_sb.rearrange("p (k w c) -> p w k c", w=WB, c=128)),
                waits=[("qWGU", lu), ("sT", p.c("sT"))], inc="sV", tag=f"f-cu-{tau}-{b}")
            st["culast"] = cu
            for w in range(WB):
                m = b * WB + w
                par = m % 2
                for k in range(KD):
                    p.add("te", (lambda w_, k_, par_: lambda e: e.matmul(
                        psA[par_][:, :TAU],
                        wgb_sb[:, (w_ * KD + k_) * 128:(w_ * KD + k_ + 1) * 128],
                        xnm_sb[:, k_ * TAU:(k_ + 1) * TAU],
                        start=(k_ == 0), stop=(k_ == KD - 1)))(w, k, par),
                        waits=[("sV", max(cg, st["t1g"][par])), ("qXM", xm)] if k == 0 else [],
                        tag=f"f-g-{tau}-{m}-{k}")
                for k in range(KD):
                    p.add("te", (lambda w_, k_, par_: lambda e: e.matmul(
                        psB[par_][:, :TAU],
                        wub_sb[:, (w_ * KD + k_) * 128:(w_ * KD + k_ + 1) * 128],
                        xnm_sb[:, k_ * TAU:(k_ + 1) * TAU],
                        start=(k_ == 0), stop=(k_ == KD - 1)))(w, k, par),
                        waits=[("sV", max(cu, st["t2g"][par]))] if k == 0 else [],
                        inc="sT" if k == KD - 1 else None, tag=f"f-u-{tau}-{m}-{k}")
                gu_d = p.c("sT")
                p.add("sc", (lambda m_, par_: lambda e: e.activation(
                    fe1_sb[:, :TAU], psA[par_][:, :TAU], AF.Exp,
                    scale=ngs_sb[:, m_:m_ + 1]))(m, par),
                    waits=[("sT", gu_d), ("sV", max(st["t1last"], st["dglast"]))],
                    tag=f"f-e1-{tau}-{m}")
                p.drain("sc", f"f1-{tau}-{m}")
                p.add("sc", lambda e: e.activation(fe2_sb[:, :TAU], fe1_sb[:, :TAU],
                                                   AF.Ln, bias=1.0), tag=f"f-e2-{tau}-{m}")
                p.drain("sc", f"f2-{tau}-{m}")
                p.add("sc", lambda e: e.activation(fe1_sb[:, :TAU], fe2_sb[:, :TAU],
                                                   AF.Exp, scale=-1.0),
                      inc="sS", tag=f"f-e3-{tau}-{m}")
                sg_d = p.c("sS")
                st["t1g"][par] = p.add("ve", (lambda m_, par_: lambda e: e.scalar_tensor_tensor(
                    ft1_sb[:, par_ * 512:par_ * 512 + TAU], psA[par_][:, :TAU],
                    gs_sb[:, m_:m_ + 1], fe1_sb[:, :TAU], OP.mult, OP.mult))(m, par),
                    waits=[("sS", sg_d), ("sG", st["hdng"][par])],
                    inc="sV", tag=f"f-t1-{tau}-{m}")
                st["t1last"] = st["t1g"][par]
                st["t2g"][par] = p.add("ve", (lambda m_, par_: lambda e: e.tensor_scalar_mul(
                    ft2_sb[:, par_ * 512:par_ * 512 + TAU], psB[par_][:, :TAU],
                    us_sb[:, m_:m_ + 1]))(m, par),
                    inc="sV", tag=f"f-t2-{tau}-{m}")
                st["hdng"][par] = p.add("gp", (lambda m_, par_: lambda e: e.tensor_mul(
                    hdn_sb[:, m_ * TAU:(m_ + 1) * TAU],
                    ft1_sb[:, par_ * 512:par_ * 512 + TAU],
                    ft2_sb[:, par_ * 512:par_ * 512 + TAU]))(m, par),
                    waits=[("sV", st["t2g"][par])] +
                          ([("sT", st["dnlast"])] if m == 0 else []),
                    inc="sG", tag=f"f-hdn-{tau}-{m}")
            yield tau
        st["gulast"] = p.c("sT")
        hdn_d = p.c("sG")
        for jg in range(8):
            for bi in range(11):
                ldd = p.add("sy", (lambda jg_, bi_: lambda e: e.dma_start(
                    out=stgd_sb.rearrange("p (q c) -> p q c", q=4),
                    in_=wd_v[:, bi_ * 4:(bi_ + 1) * 4, jg_ * 256:(jg_ + 1) * 256]))(jg, bi),
                    waits=[("sV", st["cdlast"])], inc="qWD", amt=16,
                    tag=f"f-ld-{tau}-{jg}-{bi}")
                cd = p.add("ve", lambda e: e.tensor_copy(wdb_sb[:, :], stgd_sb[:, :]),
                           waits=[("qWD", ldd), ("sT", p.c("sT"))], inc="sV",
                           tag=f"f-cd-{tau}-{jg}-{bi}")
                st["cdlast"] = cd
                for q in range(4):
                    mi = bi * 4 + q
                    for j in range(2):
                        w0 = []
                        if q == 0 and j == 0:
                            w0 = [("sV", max(cd, st["dgg"][0], st["dgg"][1])),
                                  ("sG", hdn_d)]
                        p.add("te", (lambda q_, j_, mi_, bi_: lambda e: e.matmul(
                            psD[j_][:, :TAU],
                            wdb_sb[:, q_ * 256 + j_ * 128:q_ * 256 + (j_ + 1) * 128],
                            hdn_sb[:, mi_ * TAU:(mi_ + 1) * TAU],
                            start=(bi_ == 0 and q_ == 0), stop=(bi_ == 10 and q_ == 3)))(
                                q, j, mi, bi),
                            waits=w0,
                            inc="sT" if (q == 3 and j == 1) else None,
                            tag=f"f-d-{tau}-{jg}-{bi}-{q}-{j}")
            dn_d = p.c("sT")
            st["dnlast"] = dn_d
            for j in range(2):
                jj = jg * 2 + j
                cl = p.add("ve", (lambda jj_, j_: lambda e: e.tensor_scalar(
                    dcl_sb[:, :TAU], psD[j_][:, :TAU], nds_sb[:, jj_:jj_ + 1], 30.0,
                    OP.mult, OP.min))(jj, j),
                    waits=[("sT", dn_d), ("sS", st["dsglast"])],
                    inc="sV", tag=f"f-dcl-{tau}-{jj}")
                p.add("sc", lambda e: e.activation(fe1_sb[:, :TAU], dcl_sb[:, :TAU], AF.Exp),
                      waits=[("sV", max(cl, st["dglast"]))], tag=f"f-de1-{tau}-{jj}")
                p.drain("sc", f"fd1-{tau}-{jj}")
                p.add("sc", lambda e: e.activation(fe2_sb[:, :TAU], fe1_sb[:, :TAU],
                                                   AF.Ln, bias=1.0), tag=f"f-de2-{tau}-{jj}")
                p.drain("sc", f"fd2-{tau}-{jj}")
                p.add("sc", lambda e: e.activation(fe1_sb[:, :TAU], fe2_sb[:, :TAU],
                                                   AF.Exp, scale=-1.0),
                      inc="sS", tag=f"f-de3-{tau}-{jj}")
                dsg_d = p.c("sS")
                st["dsglast"] = dsg_d
                st["dgg"][j] = p.add("ve", (lambda jj_, j_: lambda e: e.scalar_tensor_tensor(
                    dg_sb[:, j_ * 512:j_ * 512 + TAU], psD[j_][:, :TAU],
                    ds_sb[:, jj_:jj_ + 1], fe1_sb[:, :TAU], OP.mult, OP.mult))(jj, j),
                    waits=[("sS", dsg_d)] + ([st["gsp_pair"]] if st["gsp_pair"] else []),
                    inc="sV", tag=f"f-dg-{tau}-{jj}")
                st["dglast"] = st["dgg"][j]
            dgv = p.c("sV")
            gsem = f"qG{tau}"
            gv2 = p.add("gp", (lambda t_, jg_: lambda e: e.dma_start(
                out=g_v[:, jg_ * 2:(jg_ + 1) * 2, t_ * TAU:(t_ + 1) * TAU],
                in_=dg_sb.rearrange("p (k c) -> p k c", c=512)[:, :, :TAU]))(tau, jg),
                waits=[("sV", dgv)], inc=gsem, amt=16, tag=f"f-gsp-{tau}-{jg}")
            st["gsp_pair"] = (gsem, gv2)
            st["g_sp"][tau] = (gsem, gv2)
            yield tau

    def emit_final(tc):
        gsem, gsp = st["g_sp"][tc]
        for jg in range(8):
            lf = 0
            loads = [
                (fg_sb, g_v[:, jg * 2:(jg + 1) * 2, tc * TAU:(tc + 1) * TAU], False),
                (fy_sb, yout_v[tc][:, jg * 2:(jg + 1) * 2, :], False),
                (fxn_sb, None, True),
                (fx_sb, xown_v[:, jg * 2:(jg + 1) * 2, tc * TAU:(tc + 1) * TAU], False),
            ]
            lw = [("qRS", tc + 1), (gsem, gsp), ("qSPA", phA_SP),
                  ("sV", st["finlast"]), ("sG", p.c("sG")), ("qOUT", p.c("qOUT"))]
            for (buf, in_ap, dyn) in loads:
                if dyn:
                    lf = p.add("sc", (lambda b_, jg_, tc_: lambda e: e.dma_start(
                        out=b_.rearrange("p (k c) -> p k c", c=512),
                        in_=xnown_v[:, jg_ * 2:(jg_ + 1) * 2,
                                    (4 + tc_) * 512:(5 + tc_) * 512]))(buf, jg, tc),
                        waits=lw, inc="qFIN", amt=16, tag=f"n-lx-{tc}-{jg}")
                else:
                    lf = p.add("sc", (lambda b_, i_: lambda e: e.dma_start(
                        out=b_.rearrange("p (k c) -> p k c", c=512), in_=i_))(buf, in_ap),
                        waits=lw, inc="qFIN", amt=16, tag=f"n-l-{tc}-{jg}")
            for j in range(2):
                kk = jg * 2 + j
                sl = slice(j * 512, (j + 1) * 512)
                p.add("ve", (lambda kk_, sl_: lambda e: e.scalar_tensor_tensor(
                    fxn_sb[:, sl_], fxn_sb[:, sl_], d_sb[:, kk_:kk_ + 1], fy_sb[:, sl_],
                    OP.mult, OP.add))(kk, sl),
                    waits=[("qFIN", lf)] if j == 0 else [],
                    inc="sV" if j == 1 else None, tag=f"n-s-{tc}-{kk}")
            s_d = p.c("sV")
            for j in range(2):
                sl = slice(j * 512, (j + 1) * 512)
                p.add("gp", (lambda sl_: lambda e: e.tensor_mul(
                    fg_sb[:, sl_], fxn_sb[:, sl_], fg_sb[:, sl_]))(sl),
                    waits=[("sV", s_d), ("qOUT", p.c("qOUT"))] if j == 0 else [],
                    inc="sG" if j == 1 else None, tag=f"n-m-{tc}-{jg}-{j}")
            m_d = p.c("sG")
            for j in range(2):
                sl = slice(j * 512, (j + 1) * 512)
                p.add("ve", (lambda sl_: lambda e: e.tensor_add(
                    fg_sb[:, sl_], fg_sb[:, sl_], fx_sb[:, sl_]))(sl),
                    waits=[("sG", m_d)] if j == 0 else [],
                    inc="sV" if j == 1 else None, tag=f"n-a-{tc}-{jg}-{j}")
            st["finlast"] = p.c("sV")
            p.add("gp", (lambda tc_, jg_: lambda e: e.dma_start(
                out=out_v[:, jg_ * 2:(jg_ + 1) * 2, tc_ * TAU:(tc_ + 1) * TAU],
                in_=fg_sb.rearrange("p (k c) -> p k c", c=512)))(tc, jg),
                waits=[("sV", st["finlast"])], inc="qOUT", amt=16, tag=f"n-o-{tc}-{jg}")

    def drive(do_ffn, do_final):
        gs = gen_scan()
        gf = gen_ffn() if do_ffn else None
        scan_done = ffn_done = False
        last_ts = -1
        rs_done = 0
        fin_done = 0
        ffn_prog = -1
        credit = 0.0
        while not (scan_done and ffn_done):
            if not scan_done:
                try:
                    last_ts = next(gs)
                except StopIteration:
                    scan_done = True
                    last_ts = NTS - 1
            while rs_done < 4 and last_ts >= 4 + rs_done and st["ysp2"][rs_done]:
                emit_rs(rs_done)
                rs_done += 1
            credit += 1.9
            while (gf is not None) and not ffn_done and credit >= 1.0:
                try:
                    ffn_prog = next(gf)
                except StopIteration:
                    ffn_done = True
                credit -= 1.0
            if gf is None:
                ffn_done = True
            if do_final:
                while (fin_done < NTAU and rs_done > fin_done
                       and st["g_sp"][fin_done]):
                    emit_final(fin_done)
                    fin_done += 1
        while rs_done < 4:
            emit_rs(rs_done)
            rs_done += 1
        if do_final:
            while fin_done < NTAU and st["g_sp"][fin_done]:
                emit_final(fin_done)
                fin_done += 1

    if stages >= 9:
        drive(True, True)
    else:
        drive(stages >= 3, stages >= 4)

    if dbg:
        _dbglist = [("dbg_xn32", [D, T], xn32_d), ("dbg_dt", [D, T], dt_d)]
        if stages >= 2:
            for q in range(4):
                _dbglist.append((f"dbg_y{q}", [D, TS], yout_q[q].ap()))
        if stages >= 3:
            _dbglist.append(("dbg_g", [D, TO], g_d))
        for nm, shp, src_ in _dbglist:
            o = nc.dram_tensor(nm, shp, F32, kind="ExternalOutput").ap()
            dbgw = [("sG", p.c("sG")), ("sV", p.c("sV")), ("sS", p.c("sS")),
                    ("qSPA", p.c("qSPA")), ("qDTS", p.c("qDTS")), ("qRS", p.c("qRS"))]
            dbgw += [(f"qY{i}", p.c(f"qY{i}")) for i in range(4)]
            dbgw += [(f"qG{i}", p.c(f"qG{i}")) for i in range(NTAU)]
            p.add("gp", (lambda o_, s_: lambda e: e.dma_start(out=o_, in_=s_))(o, src_),
                  waits=dbgw, inc="qOUT", amt=16, tag=f"dbg-{nm}")

    p.add("gp", lambda e: e.engine_nop(), waits=[("qOUT", p.c("qOUT"))], tag="out-wait")
    mn.close()

    SEMS = sorted(set(p.cnt.keys()) | {"qOUT"})
    semstack = ExitStack()
    sem_cms = {s: semstack.enter_context(nc.semaphore(s)) for s in SEMS}

    with nc.Block() as block:
        def mk(engname):
            def body(eng):
                hw = {}
                for (e, fn, waits, incs, tag) in p.ops:
                    if e != engname:
                        continue
                    for (s, v) in waits:
                        if hw.get(s, 0) >= v:
                            continue
                        eng.wait_ge(sem_cms[s], v)
                        hw[s] = v
                    inst = fn(eng)
                    for (s, amt) in incs:
                        inst.then_inc(sem_cms[s], amt)
            return body

        block.sync(mk("sy"))
        block.scalar(mk("sc"))
        block.vector(mk("ve"))
        block.tensor(mk("te"))
        block.gpsimd(mk("gp"))

    semstack.close()
    perst.close()
    return nc


def _prep_inputs(L0_attn_x, L0_attn_A_log, L0_attn_D, L0_attn_W_xproj,
                 L0_attn_W_dt, L0_attn_b_dt, L0_ffn_gate_w, L0_ffn_up_w,
                 L0_ffn_down_w, L0_ffn_gate_s, L0_ffn_up_s, L0_ffn_down_s):
    f32 = np.float32
    x = np.asarray(L0_attn_x, f32)
    wxpT = np.asarray(L0_attn_W_xproj, f32).T
    alog = np.asarray(L0_attn_A_log, f32)
    shared = {
        "wdt_t": np.ascontiguousarray(np.asarray(L0_attn_W_dt, f32).T),
        "wg_t": np.ascontiguousarray(np.asarray(L0_ffn_gate_w, f32).T),
        "wu_t": np.ascontiguousarray(np.asarray(L0_ffn_up_w, f32).T),
        "wd_t": np.ascontiguousarray(np.asarray(L0_ffn_down_w, f32).T),
        "d_r": np.ascontiguousarray(np.asarray(L0_attn_D, f32).reshape(KD, 128).T),
        "bdt_r": np.ascontiguousarray(np.asarray(L0_attn_b_dt, f32).reshape(KD, 128).T),
        "gs_r": np.ascontiguousarray(np.asarray(L0_ffn_gate_s, f32).reshape(MI, 128).T),
        "us_r": np.ascontiguousarray(np.asarray(L0_ffn_up_s, f32).reshape(MI, 128).T),
        "ds_r": np.ascontiguousarray(np.asarray(L0_ffn_down_s, f32).reshape(KD, 128).T),
        "ident": np.eye(128, dtype=f32),
        "consts_r": np.ascontiguousarray(np.tile(
            np.array([1e-5, 1.0, -0.5, -1.0, 1.0 / 2048, 30.0, 0.0, 0.0], f32),
            (128, 1))),
    }
    in_maps = []
    for c in range(8):
        b, h = c // 2, c % 2
        im = dict(shared)
        im["x_sc"] = np.ascontiguousarray(x[b])
        im["x_own"] = np.ascontiguousarray(x[b, h * TO:(h + 1) * TO].T)
        im["wxp_t"] = np.ascontiguousarray(np.concatenate(
            [wxpT[:, :R], wxpT[:, R + NH * h:R + NH * (h + 1)],
             wxpT[:, R + NST + NH * h:R + NST + NH * (h + 1)]], axis=1))
        im["alog_r"] = np.ascontiguousarray(
            alog[:, NH * h:NH * (h + 1)].reshape(KD, 128, NH)
            .transpose(1, 0, 2).reshape(128, KD * NH))
        im["off_r"] = np.array([[h * NTAU]], np.uint32)
        im["off2_r"] = np.array([[4 - h * 4]], np.uint32)
        in_maps.append(im)
    return in_maps


def kernel(**inputs):
    in_maps = _prep_inputs(**inputs)
    nc = build()
    res = run_bass_kernel_spmd(nc, in_maps, core_ids=list(range(8)))
    out = np.empty((4, T, D), np.float32)
    for c in range(8):
        b, h = c // 2, c % 2
        out[b, h * TO:(h + 1) * TO] = res.results[c]["out_ft"].T
    return out


# version 04
# speedup vs baseline: 1.1346x; 1.1346x over previous
import numpy as np
from contextlib import ExitStack
import concourse.bass as bass
import concourse.mybir as mybir
from concourse.bass_utils import run_bass_kernel_spmd

F32 = mybir.dt.float32
BF16 = mybir.dt.bfloat16
U32 = mybir.dt.uint32
AF = mybir.ActivationFunctionType
OP = mybir.AluOpType
AX = mybir.AxisListType

T, TO, D, KD, R, I, MI = 4096, 2048, 2048, 16, 64, 5632, 44
NST = 16
NH = 8
EPS = 1e-5
TS = 512
NTS = T // TS
KQ = 2
NQ = KD // KQ
NG = 2
NGR = NH // NG
TAU = 512
NTAU = TO // TAU
WB = 2
NB = MI // WB
RG = [[0, 1], [2, 3], [4, 5], [6, 7]]


class Plan:
    def __init__(self):
        self.ops = []
        self.cnt = {}
        self.pend = {}

    def c(self, sem):
        return self.cnt.get(sem, 0)

    def add(self, eng, fn, waits=(), inc=None, amt=1, tag=""):
        w = list(self.pend.pop(eng, []))
        for s, v in waits:
            assert v <= self.cnt.get(s, 0), f"wait-ahead {s}:{v}>{self.cnt.get(s,0)} @{tag}"
            if v > 0:
                w.append((s, v))
        incs = []
        if inc is not None:
            self.cnt[inc] = self.cnt.get(inc, 0) + amt
            incs.append((inc, amt))
        self.ops.append([eng, fn, tuple(w), incs, tag])
        return self.cnt.get(inc, 0) if inc is not None else None

    def drain(self, eng, tag=""):
        self.add(eng, lambda e: e.drain(), tag=f"drain-{tag}")

    def wait_next(self, eng, sem, val):
        self.pend.setdefault(eng, []).append((sem, val))


def build(dbg=False, stages=9):
    nc = bass.Bass("TRN2", target_bir_lowering=False)

    x_sc = nc.dram_tensor("x_sc", [T, D], F32, kind="ExternalInput").ap()
    x_own = nc.dram_tensor("x_own", [D, TO], F32, kind="ExternalInput").ap()
    wxp_t = nc.dram_tensor("wxp_t", [D, 80], F32, kind="ExternalInput").ap()
    wdt_t = nc.dram_tensor("wdt_t", [R, D], F32, kind="ExternalInput").ap()
    wg_t = nc.dram_tensor("wg_t", [D, I], F32, kind="ExternalInput").ap()
    wu_t = nc.dram_tensor("wu_t", [D, I], F32, kind="ExternalInput").ap()
    wd_t = nc.dram_tensor("wd_t", [I, D], F32, kind="ExternalInput").ap()
    alog_r = nc.dram_tensor("alog_r", [128, KD * NH], F32, kind="ExternalInput").ap()
    d_r = nc.dram_tensor("d_r", [128, KD], F32, kind="ExternalInput").ap()
    bdt_r = nc.dram_tensor("bdt_r", [128, KD], F32, kind="ExternalInput").ap()
    gs_r = nc.dram_tensor("gs_r", [128, MI], F32, kind="ExternalInput").ap()
    us_r = nc.dram_tensor("us_r", [128, MI], F32, kind="ExternalInput").ap()
    ds_r = nc.dram_tensor("ds_r", [128, KD], F32, kind="ExternalInput").ap()
    ident = nc.dram_tensor("ident", [128, 128], F32, kind="ExternalInput").ap()
    consts_r = nc.dram_tensor("consts_r", [128, 8], F32, kind="ExternalInput").ap()
    off_r = nc.dram_tensor("off_r", [1, 1], U32, kind="ExternalInput").ap()
    off2_r = nc.dram_tensor("off2_r", [1, 1], U32, kind="ExternalInput").ap()
    out_ft = nc.dram_tensor("out_ft", [D, TO], F32, kind="ExternalOutput").ap()

    xn32_d = nc.dram_tensor("xn32_d", [D, T], F32).ap()
    xnbf_d = nc.dram_tensor("xnbf_d", [D, T], BF16).ap()
    dt_d = nc.dram_tensor("dt_d", [D, T], F32).ap()
    xpjbf_d = nc.dram_tensor("xpjbf_d", [2 * NH, T], BF16).ap()
    yin_q = [nc.dram_tensor(f"yin{q}", [2, D, TS], F32) for q in range(4)]
    yout_q = [nc.dram_tensor(f"yout{q}", [D, TS], F32) for q in range(4)]
    g_d = nc.dram_tensor("g_d", [D, TO], F32).ap()
    xnown_d = nc.dram_tensor("xnown_d", [D, 12 * 512], F32).ap()
    wgbf_d = nc.dram_tensor("wgbf_d", [NB, 128, WB * KD * 128], BF16)
    wubf_d = nc.dram_tensor("wubf_d", [NB, 128, WB * KD * 128], BF16)
    wdbf_d = nc.dram_tensor("wdbf_d", [8, 11, 128, 4 * 256], BF16)

    p = Plan()

    rr = lambda ap_: ap_.rearrange("(k p) t -> p k t", p=128)
    xn32_v, xnbf_v, dt_v, g_v = rr(xn32_d), rr(xnbf_d), rr(dt_d), rr(g_d)
    xnown_v = rr(xnown_d)
    out_v, xown_v = rr(out_ft), rr(x_own)
    yin_v = [y.ap().rearrange("b (k p) t -> p b k t", p=128) for y in yin_q]
    yout_v = [y.ap().rearrange("(k p) t -> p k t", p=128) for y in yout_q]
    wg_v = wg_t.rearrange("(k p) i -> p k i", p=128)
    wu_v = wu_t.rearrange("(k p) i -> p k i", p=128)
    wd_v = wd_t.rearrange("(q p) h -> p q h", p=128)
    wxp_v = wxp_t.rearrange("(k p) e -> p k e", p=128)

    perst = ExitStack()
    SB = lambda n, s, d=F32: perst.enter_context(nc.sbuf_tensor(n, s, d))
    a_sb = SB("a_sb", [128, KD * NH])
    d_sb = SB("d_sb", [128, KD])
    bdt_sb = SB("bdt_sb", [128, KD])
    gs_sb = SB("gs_sb", [128, MI])
    ngs_sb = SB("ngs_sb", [128, MI])
    us_sb = SB("us_sb", [128, MI])
    ds_sb = SB("ds_sb", [128, KD])
    nds_sb = SB("nds_sb", [128, KD])
    hcar_sb = SB("hcar_sb", [128, KD * NH])
    off_sb = SB("off_sb", [1, 1], U32)
    off2_sb = SB("off2_sb", [1, 1], U32)
    cst_sb = SB("cst_sb", [128, 8])
    for _i, _cv in enumerate([EPS, 1.0, -0.5, -1.0, 1.0 / D, 30.0]):
        nc.const_aps.register(F32, _cv, cst_sb[:, _i:_i + 1])

    psA = [perst.enter_context(nc.psum_tensor(f"psA{i}", [128, 512], F32)) for i in range(2)]
    psB = [perst.enter_context(nc.psum_tensor(f"psB{i}", [128, 512], F32)) for i in range(2)]
    psD = [perst.enter_context(nc.psum_tensor(f"psD{i}", [128, 512], F32)) for i in range(2)]

    HOLD = {}

    phA = ExitStack()
    SA = lambda n, s, d=F32: phA.enter_context(nc.sbuf_tensor(n, s, d))
    idf_sb = SA("idf_sb", [128, 128])
    wxp_sb = SA("wxp_sb", [128, KD * 80], BF16)
    wdt_sb = SA("wdt_sb", [R, D], BF16)
    wxpf_sb = SA("wxpf_sb", [128, KD * 80])
    wdtf_sb = SA("wdtf_sb", [R, D])
    xtm = [SA(f"xtm{i}", [128, D]) for i in range(2)]
    x2_sb = SA("x2_sb", [128, D], BF16)
    sst = SA("sst", [128, 8])
    xntm_sb = SA("xntm_sb", [128, D])
    xnft_sb = SA("xnft_sb", [128, KD * 512])
    xnbf_sb = SA("xnbf_sb", [128, KD * 512], BF16)
    xpj_sb = SA("xpj_sb", [80, T])
    dtp_sb = SA("dtp_sb", [R, T], BF16)
    dtq_sb = SA("dtq_sb", [128, 4 * 512])
    spt_sb = SA("spt_sb", [128, 512])
    bcst_sb = SA("bcst_sb", [2 * NH, T], BF16)

    for name, dst, src in [
        ("alog", a_sb[:, :], alog_r), ("d", d_sb[:, :], d_r), ("bdt", bdt_sb[:, :], bdt_r),
        ("gs", gs_sb[:, :], gs_r), ("us", us_sb[:, :], us_r), ("ds", ds_sb[:, :], ds_r),
        ("idf", idf_sb[:, :], ident), ("csts", cst_sb[:, :], consts_r),
        ("off", off_sb[:, :], off_r), ("off2", off2_sb[:, :], off2_r),
        ("wxpf", wxpf_sb.rearrange("p (k e) -> p k e", e=80), wxp_v),
        ("wdtf", wdtf_sb[:, :], wdt_t),
    ]:
        p.add("sy", (lambda d_, s_: lambda e: e.dma_start(out=d_, in_=s_))(dst, src),
              inc="qPRE", amt=16, tag=f"pre-{name}")
    preLD = p.c("qPRE")

    p.add("sc", lambda e: e.activation(a_sb[:, :], a_sb[:, :], AF.Exp),
          waits=[("qPRE", preLD)], inc="sS", tag="pre-expA")
    p.add("ve", lambda e: e.tensor_scalar_mul(a_sb[:, :], a_sb[:, :], -1.0),
          waits=[("sS", p.c("sS"))], tag="pre-negA")
    p.add("ve", lambda e: e.tensor_scalar_mul(ngs_sb[:, :], gs_sb[:, :], -1.0),
          waits=[("qPRE", preLD)], tag="pre-neggs")
    p.add("ve", lambda e: e.tensor_scalar_mul(nds_sb[:, :], ds_sb[:, :], -1.0), tag="pre-negds")
    p.add("ve", lambda e: e.tensor_copy(wxp_sb[:, :], wxpf_sb[:, :]), tag="pre-cwxp")
    p.add("ve", lambda e: e.tensor_copy(wdt_sb[:, :], wdtf_sb[:, :]), tag="pre-cwdt")
    p.add("ve", lambda e: e.memset(hcar_sb[:, :], 0.0), inc="sV", tag="pre-hcar")
    preV = p.c("sV")

    SUM, SSQ, M_, MSQ, VAR, LV, RSTD = range(7)
    sv_ = lambda i: sst[:, i:i + 1]
    ISQD = float(1.0 / np.sqrt(np.float64(D)))

    for sc_i in range(8):
        for c4 in range(4):
            ci = sc_i * 4 + c4
            buf = xtm[ci % 2]
            xl = p.add("sy", (lambda b_, r0: lambda e: e.dma_start(
                out=b_[:, :], in_=x_sc[r0:r0 + 128, :]))(buf, ci * 128),
                waits=[("sV", p.c("sV")), ("sS", p.c("sS"))] if ci >= 2 else [],
                inc=f"qXC{ci % 2}", amt=16, tag=f"A-x-{ci}")
            p.add("ve", (lambda b_: lambda e: e.reduce_sum(sv_(SUM), b_[:, :], AX.X))(buf),
                  waits=[(f"qXC{ci % 2}", xl), ("sG", p.c("sG")), ("sS", p.c("sS"))],
                  inc="sV", tag=f"A-sum-{ci}")
            sum_d = p.c("sV")
            p.add("sc", (lambda b_: lambda e: e.activation(
                x2_sb[:, :], b_[:, :], AF.Square, scale=ISQD))(buf),
                waits=[(f"qXC{ci % 2}", xl), ("sV", p.c("sV"))], inc="sS", tag=f"A-sq-{ci}")
            sq_d = p.c("sS")
            p.add("ve", lambda e: e.reduce_sum(sv_(SSQ), x2_sb[:, :], AX.X),
                  waits=[("sS", sq_d), ("sG", p.c("sG"))], inc="sV", tag=f"A-ssq-{ci}")
            ssq_d = p.c("sV")
            p.add("sc", lambda e: e.activation(sv_(M_), sv_(SUM), AF.Copy, scale=1.0 / D),
                  waits=[("sV", sum_d)], tag=f"A-m-{ci}")
            p.add("sc", lambda e: e.activation(sv_(MSQ), sv_(SUM), AF.Square, scale=1.0 / D),
                  inc="sS", tag=f"A-msq-{ci}")
            msq_d = p.c("sS")
            p.add("gp", lambda e: e.tensor_sub(sv_(VAR), sv_(SSQ), sv_(MSQ)),
                  waits=[("sV", ssq_d), ("sS", msq_d)], inc="sG", tag=f"A-var-{ci}")
            var_d = p.c("sG")
            p.add("sc", lambda e: e.activation(sv_(LV), sv_(VAR), AF.Ln, bias=EPS),
                  waits=[("sG", var_d)], tag=f"A-lv-{ci}")
            p.drain("sc", f"Alv-{ci}")
            p.add("sc", lambda e: e.activation(sv_(RSTD), sv_(LV), AF.Exp, scale=-0.5),
                  inc="sS", tag=f"A-rstd-{ci}")
            rstd_d = p.c("sS")
            p.add("ve", (lambda b_: lambda e: e.tensor_scalar(
                xntm_sb[:, :], b_[:, :], sv_(M_), sv_(RSTD), OP.subtract, OP.mult))(buf),
                waits=[("sS", rstd_d), ("sT", p.c("sT"))], inc="sV", tag=f"A-nrm-{ci}")
            for kg in range(4):
                for j in range(4):
                    k = kg * 4 + j
                    p.add("te", (lambda k_, j_: lambda e: e.transpose(
                        psA[0][:, j_ * 128:(j_ + 1) * 128],
                        xntm_sb[:, k_ * 128:(k_ + 1) * 128], idf_sb[:, :]))(k, j),
                        waits=[("sV", p.c("sV"))] if j == 0 else [],
                        inc="sT" if j == 3 else None, tag=f"A-tr-{ci}-{k}")
                tr_d = p.c("sT")
                dst = xnft_sb.rearrange("p (k c) -> p k c", c=512)[
                    :, kg * 4:(kg + 1) * 4, c4 * 128:(c4 + 1) * 128]
                p.add("ve", (lambda d_: lambda e: e.tensor_copy(
                    d_, psA[0].rearrange("p (k c) -> p k c", c=128)))(dst),
                    waits=[("sT", tr_d)] + ([("qSPA", p.c("qSPA")), ("sS", p.c("sS")),
                                             ("qSPO", p.c("qSPO"))]
                                            if kg == 0 and c4 == 0 else []),
                    inc="sV", tag=f"A-ev-{ci}-{kg}")
        ev_all = p.c("sV")
        p.add("sc", lambda e: e.activation(xnbf_sb[:, :], xnft_sb[:, :], AF.Copy),
              waits=[("sV", ev_all), ("qSPA", p.c("qSPA")), ("sT", p.c("sT"))],
              inc="sS", tag=f"A-cast-{sc_i}")
        cast_d = p.c("sS")
        p.add("gp", (lambda s_: lambda e: e.dma_start(
            out=xn32_v[:, :, s_ * 512:(s_ + 1) * 512],
            in_=xnft_sb.rearrange("p (k c) -> p k c", c=512)))(sc_i),
            waits=[("sV", ev_all)], inc="qSPA", amt=16, tag=f"A-sp32-{sc_i}")
        p.add("gp", (lambda s_: lambda e: e.dma_start(
            out=xnbf_v[:, :, s_ * 512:(s_ + 1) * 512],
            in_=xnbf_sb.rearrange("p (k c) -> p k c", c=512)))(sc_i),
            waits=[("sS", cast_d)], inc="qSPA", amt=16, tag=f"A-spbf-{sc_i}")

        def _ownspill(s_):
            def fn(e):
                sv = e.value_load(off2_sb[0:1, 0:1])
                return e.dma_start(
                    out=xnown_v[:, :, bass.ts(sv + s_, 512)],
                    in_=xnft_sb.rearrange("p (k c) -> p k c", c=512))
            return fn
        p.add("sy", _ownspill(sc_i), waits=[("sV", ev_all), ("qPRE", preLD)],
              inc="qSPO", amt=16, tag=f"A-spown-{sc_i}")
        for k in range(KD):
            p.add("te", (lambda k_: lambda e: e.matmul(
                psA[1][0:80, :], wxp_sb[:, k_ * 80:(k_ + 1) * 80],
                xnbf_sb[:, k_ * 512:(k_ + 1) * 512],
                start=(k_ == 0), stop=(k_ == KD - 1)))(k),
                waits=[("sS", cast_d), ("sV", preV)] if k == 0 else [],
                inc="sT" if k == KD - 1 else None, tag=f"A-xp-{sc_i}-{k}")
        xp_d = p.c("sT")
        p.add("sc", (lambda s_: lambda e: e.copy(
            xpj_sb[:, s_ * 512:(s_ + 1) * 512], psA[1][0:80, :]))(sc_i),
            waits=[("sT", xp_d)], tag=f"A-xpev-{sc_i}")
        p.drain("sc", f"Axpev-{sc_i}")
        p.add("sc", (lambda s_: lambda e: e.activation(
            dtp_sb[:, s_ * 512:(s_ + 1) * 512], xpj_sb[0:R, s_ * 512:(s_ + 1) * 512],
            AF.Copy))(sc_i), inc="sS", tag=f"A-dtp-{sc_i}")
        dtp_d = p.c("sS")
        for k in range(KD):
            par = k % 2
            p.add("te", (lambda s_, k_, par_: lambda e: e.matmul(
                psD[par_][:, :], wdt_sb[:, k_ * 128:(k_ + 1) * 128],
                dtp_sb[:, s_ * 512:(s_ + 1) * 512], start=True, stop=True))(sc_i, k, par),
                waits=[("sS", dtp_d if k == 0 else p.c("sS"))],
                inc="sT", tag=f"A-dtm-{sc_i}-{k}")
            dtm_d = p.c("sT")
            p.add("sc", (lambda par_, k_: lambda e: e.activation(
                spt_sb[:, :], psD[par_][:, :], AF.Exp, bias=bdt_sb[:, k_:k_ + 1]))(par, k),
                waits=[("sT", dtm_d)] + ([("qDTS", p.c("qDTS"))] if k % 4 == 0 else []),
                tag=f"A-spe-{sc_i}-{k}")
            p.drain("sc", f"Asp-{sc_i}-{k}")
            p.add("sc", (lambda k_: lambda e: e.activation(
                dtq_sb[:, (k_ % 4) * 512:(k_ % 4 + 1) * 512], spt_sb[:, :],
                AF.Ln, bias=1.0))(k), inc="sS", tag=f"A-spl-{sc_i}-{k}")
            if k % 4 == 3:
                p.add("gp", (lambda s_, kq_: lambda e: e.dma_start(
                    out=dt_v[:, kq_ * 4:(kq_ + 1) * 4, s_ * 512:(s_ + 1) * 512],
                    in_=dtq_sb.rearrange("p (k c) -> p k c", c=512)))(sc_i, k // 4),
                    waits=[("sS", p.c("sS"))], inc="qDTS", amt=16, tag=f"A-dtsp-{sc_i}-{k}")
    p.add("sc", lambda e: e.activation(bcst_sb[:, :], xpj_sb[R:R + 2 * NH, :], AF.Copy),
          inc="sS", tag="A-bccast")
    spbc = p.add("gp", lambda e: e.dma_start(out=xpjbf_d, in_=bcst_sb[:, :]),
                 waits=[("sS", p.c("sS"))], inc="qSPA", amt=16, tag="A-spbc")
    phA_SP = p.c("qSPA")

    barw = [("sV", p.c("sV")), ("sS", p.c("sS")), ("sT", p.c("sT")), ("sG", p.c("sG")),
            ("qSPA", p.c("qSPA")), ("qDTS", p.c("qDTS")), ("qPRE", p.c("qPRE")),
            ("qSPO", p.c("qSPO")), ("qXC0", p.c("qXC0")), ("qXC1", p.c("qXC1"))]
    for eng in ("ve", "sc", "te", "gp", "sy"):
        for s, v in barw:
            p.wait_next(eng, s, v)
    phA.close()

    mn = ExitStack()
    SM = lambda n, s, d=F32: mn.enter_context(nc.sbuf_tensor(n, s, d))
    dtk_sb = SM("dtk_sb", [128, KQ * TS])
    xnk_sb = SM("xnk_sb", [128, KQ * TS])
    ab_sb = SM("ab_sb", [128, 2 * NG * TS])
    ub_sb = SM("ub_sb", [128, 2 * NG * TS])
    hb_sb = SM("hb_sb", [128, 2 * NG * TS])
    pr_sb = SM("pr_sb", [128, 2 * TS * NG])
    yk_sb = SM("yk_sb", [128, KQ * TS])
    yk2_sb = SM("yk2_sb", [128, KQ * TS])
    bbc_sb = SM("bbc_sb", [128, NGR * NG * TS], BF16)
    cbc_sb = SM("cbc_sb", [128, NGR * NG * TS], BF16)
    stg_sb = SM("stg_sb", [128, KD * WB * 128])
    wgb_sb = SM("wgb_sb", [128, WB * KD * 128], BF16)
    wub_sb = SM("wub_sb", [128, WB * KD * 128], BF16)
    stgd_sb = SM("stgd_sb", [128, 4 * 256])
    wdb_sb = SM("wdb_sb", [128, 4 * 256], BF16)
    hdn_sb = SM("hdn_sb", [128, MI * TAU], BF16)
    xnm_sb = SM("xnm_sb", [128, KD * TAU], BF16)
    fe1_sb = SM("fe1_sb", [128, 512])
    fe2_sb = SM("fe2_sb", [128, 512])
    ft1_sb = SM("ft1_sb", [128, 2 * 512])
    ft2_sb = SM("ft2_sb", [128, 2 * 512])
    dcl_sb = SM("dcl_sb", [128, 512])
    dg_sb = SM("dg_sb", [128, 2 * 512])
    fg_sb = SM("fg_sb", [128, 2 * 512])
    fy_sb = SM("fy_sb", [128, 2 * 512])
    fxn_sb = SM("fxn_sb", [128, 2 * 512])
    fx_sb = SM("fx_sb", [128, 2 * 512])

    st = dict(gcnt=0, dlv=0, zlast=0,
              ttspar=[0, 0], prodpar=[0, 0], carpar=[0, 0], redpar=[0, 0],
              ubc=[0] * NGR, pbc=[0] * NGR, bc={},
              yk_pair=None, yk2_pair=None, ysp2=[None] * 4,
              t1g=[0, 0], t2g=[0, 0], hdng=[0, 0], dgg=[0, 0], dglast=0,
              t1last=0, culast=0, cdlast=0, gulast=0, dnlast=0, dsglast=0,
              finlast=0, gsp_pair=None, g_sp=[None] * NTAU, wsp_all=0)

    bbc_v = bbc_sb.rearrange("p (q n c) -> p q n c", q=NGR, c=TS)
    cbc_v = cbc_sb.rearrange("p (q n c) -> p q n c", q=NGR, c=TS)
    abv = ab_sb.rearrange("p (q n c) -> p q n c", q=2, c=TS)
    ubv = ub_sb.rearrange("p (q n c) -> p q n c", q=2, c=TS)
    hbv = hb_sb.rearrange("p (q n c) -> p q n c", q=2, c=TS)
    prv = pr_sb.rearrange("p (q c n) -> p q c n", q=2, n=NG)

    def emit_bcast(ts, gi):
        c0 = ts * TS
        nb = gi * NG
        p.add("sc", (lambda nb_, gi_, c0_: lambda e: e.dma_start(
            out=bbc_v[:, gi_, :, :],
            in_=xpjbf_d[nb_:nb_ + NG, c0_:c0_ + TS].unsqueeze(0).broadcast_to(
                [128, NG, TS])))(nb, gi, c0),
            waits=[("sG", st["ubc"][gi]), ("qSPA", spbc)], inc=f"qBC{gi}", amt=16,
            tag=f"s-bb-{ts}-{gi}")
        cb = p.add("sc", (lambda nb_, gi_, c0_: lambda e: e.dma_start(
            out=cbc_v[:, gi_, :, :],
            in_=xpjbf_d[NH + nb_:NH + nb_ + NG, c0_:c0_ + TS].unsqueeze(0).broadcast_to(
                [128, NG, TS])))(nb, gi, c0),
            waits=[("sG", st["pbc"][gi])], inc=f"qBC{gi}", amt=16, tag=f"s-cb-{ts}-{gi}")
        st["bc"][gi] = cb

    def gen_scan():
      for ts in range(NTS):
        c0 = ts * TS
        q4 = ts % 4
        hh = ts // 4
        for _gi in range(NGR):
            emit_bcast(ts, _gi)
        for kq in range(NQ):
            xl = p.add("sy", (lambda kq_, c0_: lambda e: e.dma_start(
                out=xnk_sb.rearrange("p (k c) -> p k c", c=TS),
                in_=xn32_v[:, kq_ * KQ:(kq_ + 1) * KQ, c0_:c0_ + TS]))(kq, c0),
                waits=[("qSPA", phA_SP), ("sG", p.c("sG"))],
                inc="qXL", amt=16, tag=f"s-xl-{ts}-{kq}")
            dl = p.add("sy", (lambda kq_, c0_: lambda e: e.dma_start(
                out=dtk_sb.rearrange("p (k c) -> p k c", c=TS),
                in_=dt_v[:, kq_ * KQ:(kq_ + 1) * KQ, c0_:c0_ + TS]))(kq, c0),
                waits=[("sS", p.c("sS")), ("sG", st["zlast"])],
                inc="qDL", amt=16, tag=f"s-dl-{ts}-{kq}")
            st["dlv"] = dl
            for ki in range(KQ):
                p.add("gp", (lambda ki_: lambda e: e.tensor_mul(
                    xnk_sb[:, ki_ * TS:(ki_ + 1) * TS], dtk_sb[:, ki_ * TS:(ki_ + 1) * TS],
                    xnk_sb[:, ki_ * TS:(ki_ + 1) * TS]))(ki),
                    waits=[("qDL", dl), ("qXL", xl)] if ki == 0 else [],
                    inc="sG" if ki == KQ - 1 else None, tag=f"s-z-{ts}-{kq}-{ki}")
            p.drain("gp", f"z-{ts}-{kq}")
            st["zlast"] = p.c("sG")
            def stage_front(ki, gi):
                k = kq * KQ + ki
                nb = gi * NG
                cb_v = st["bc"][gi]
                par = st["gcnt"] % 2
                st["gcnt"] += 1
                for n in range(NG):
                    p.add("sc", (lambda ki_, n_, k_, nb_, par_: lambda e: e.activation(
                        abv[:, par_, n_, :], dtk_sb[:, ki_ * TS:(ki_ + 1) * TS],
                        AF.Exp,
                        scale=a_sb[:, k_ * NH + nb_ + n_:k_ * NH + nb_ + n_ + 1]))(
                            ki, n, k, nb, par),
                        waits=[("sV", st["ttspar"][par]), ("qDL", st["dlv"])]
                        if n == 0 else [],
                        inc="sS" if n == NG - 1 else None, tag=f"s-e-{ts}-{k}-{gi}-{n}")
                exp_d = p.c("sS")
                for n in range(NG):
                    p.add("gp", (lambda ki_, n_, par_, gi_: lambda e: e.tensor_mul(
                        ubv[:, par_, n_, :], xnk_sb[:, ki_ * TS:(ki_ + 1) * TS],
                        bbc_v[:, gi_, n_, :]))(ki, n, par, gi),
                        waits=[(f"qBC{gi}", cb_v), ("sV", st["ttspar"][par])]
                        if n == 0 else [],
                        inc="sG" if n == NG - 1 else None, tag=f"s-u-{ts}-{k}-{gi}-{n}")
                u_d = p.c("sG")
                st["ubc"][gi] = u_d
                for n in range(NG):
                    col = k * NH + nb + n
                    p.add("ve", (lambda n_, col_, par_: lambda e: e.tensor_tensor_scan(
                        hbv[:, par_, n_, :], abv[:, par_, n_, :], ubv[:, par_, n_, :],
                        hcar_sb[:, col_:col_ + 1], OP.mult, OP.add))(n, col, par),
                        waits=[("sS", exp_d), ("sG", u_d),
                               ("sG", st["prodpar"][par]), ("sS", st["carpar"][par])]
                        if n == 0 else [],
                        inc="sV" if n == NG - 1 else None, tag=f"s-h-{ts}-{k}-{gi}-{n}")
                st["ttspar"][par] = p.c("sV")
                return (ki, gi, k, nb, par, st["ttspar"][par])

            def stage_back(job):
                ki, gi, k, nb, par, tts_d = job
                p.add("sc", (lambda k_, nb_, par_: lambda e: e.copy(
                    hcar_sb[:, k_ * NH + nb_:k_ * NH + nb_ + NG],
                    hbv[:, par_, :, TS - 1:TS]))(k, nb, par),
                    waits=[("sV", tts_d)], inc="sS", tag=f"s-c-{ts}-{k}-{gi}")
                st["carpar"][par] = p.c("sS")
                for n in range(NG):
                    p.add("gp", (lambda n_, par_, gi_: lambda e: e.tensor_mul(
                        prv[:, par_, :, n_:n_ + 1],
                        hbv[:, par_, n_, :].unsqueeze(2),
                        cbc_v[:, gi_, n_, :].unsqueeze(2)))(n, par, gi),
                        waits=[("sV", tts_d), ("sV", st["redpar"][par])]
                        if n == 0 else [],
                        inc="sG" if n == NG - 1 else None, tag=f"s-p-{ts}-{k}-{gi}-{n}")
                prod_d = p.c("sG")
                st["prodpar"][par] = prod_d
                st["pbc"][gi] = prod_d
                return prod_d

            def stage_red(job, prod_d):
                ki, gi, k, nb, par, tts_d = job
                ykb = yk_sb if gi == 1 else yk2_sb
                ypair = st["yk_pair"] if gi == 1 else st["yk2_pair"]
                pr4 = pr_sb.rearrange("p (q c n) -> p c q n", q=2, n=NG)
                p.add("ve", (lambda ki_, yb_: lambda e: e.tensor_reduce(
                    yb_[:, ki_ * TS:(ki_ + 1) * TS].unsqueeze(2).unsqueeze(3),
                    pr4, AX.XY, OP.add))(ki, ykb),
                    waits=[("sG", prod_d)] + ([ypair] if ypair and ki == 0 else []),
                    inc="sV", tag=f"s-r-{ts}-{k}-{gi}")
                st["redpar"][0] = st["redpar"][1] = p.c("sV")

            pend = None
            pend_red = None
            for ki in range(KQ):
                for gi in range(NGR):
                    job = stage_front(ki, gi)
                    if pend_red is not None:
                        stage_red(*pend_red)
                        pend_red = None
                    if pend is not None:
                        pd = stage_back(pend)
                        if pend[1] % 2 == 1:
                            pend_red = (pend, pd)
                    pend = job
            if pend_red is not None:
                stage_red(*pend_red)
                pend_red = None
            if pend is not None:
                pd = stage_back(pend)
                if pend[1] % 2 == 1:
                    stage_red(pend, pd)
            ysem = f"qY{q4}"
            allred = p.c("sV")
            y1 = p.add("gp", (lambda kq_, q_, hh_: lambda e: e.dma_start(
                out=yin_v[q_][:, hh_, kq_ * KQ:(kq_ + 1) * KQ, :],
                in_=yk_sb.rearrange("p (k c) -> p k c", c=TS)))(kq, q4, hh),
                waits=[("sV", allred)], inc=ysem, amt=16, tag=f"s-y1-{ts}-{kq}")
            st["yk_pair"] = (ysem, y1)
            y2 = p.add("gp", (lambda kq_, q_, hh_: lambda e: e.dma_start(
                out=yin_v[q_][:, hh_, kq_ * KQ:(kq_ + 1) * KQ, :],
                in_=yk2_sb.rearrange("p (k c) -> p k c", c=TS),
                accum_op=OP.add))(kq, q4, hh),
                waits=[(ysem, y1)], inc=ysem, amt=16, tag=f"s-y2-{ts}-{kq}")
            st["yk2_pair"] = (ysem, y2)
            yield ts
        if hh == 1:
            st["ysp2"][q4] = st["yk2_pair"]

    def emit_rs(q):
        ysem, yv = st["ysp2"][q]
        p.add("gp", (lambda q_: lambda e: e.collective_compute(
            "ReduceScatter", OP.add, replica_groups=RG,
            ins=[yin_q[q_].ap()], outs=[yout_q[q_].ap()]))(q),
            waits=[(ysem, yv)], inc="qRS", tag=f"rs-{q}")

    def gen_ffn():
      for tau in range(NTAU):
        def _xmfn(t_):
            def fn(e):
                sv = e.value_load(off_sb[0:1, 0:1])
                return e.dma_start(
                    out=xnm_sb.rearrange("p (k c) -> p k c", c=TAU),
                    in_=xnbf_v[:, :, bass.ts(sv + t_, TAU)])
            return fn
        xm = p.add("sy", _xmfn(tau),
                   waits=[("qSPA", phA_SP), ("sT", st["gulast"]), ("qPRE", preLD)],
                   inc="qXM", amt=16, tag=f"f-xm-{tau}")
        for b in range(NB):
            if tau == 0:
                lg = p.add("sy", (lambda b_: lambda e: e.dma_start(
                    out=stg_sb.rearrange("p (k w c) -> p k (w c)", k=KD, w=WB),
                    in_=wg_v[:, :, b_ * WB * 128:(b_ + 1) * WB * 128]))(b),
                    waits=[("sV", st["culast"])], inc="qWGU", amt=16, tag=f"f-lg-{tau}-{b}")
                cg = p.add("ve", lambda e: e.tensor_copy(
                    wgb_sb.rearrange("p (w k c) -> p w k c", w=WB, c=128),
                    stg_sb.rearrange("p (k w c) -> p w k c", w=WB, c=128)),
                    waits=[("qWGU", lg), ("sT", p.c("sT")), ("qWS", p.c("qWS"))],
                    inc="sV", tag=f"f-cg-{tau}-{b}")
                p.add("sy", (lambda b_: lambda e: e.dma_start(
                    out=wgbf_d.ap()[b_], in_=wgb_sb[:, :]))(b),
                    waits=[("sV", cg)], inc="qWS", amt=16, tag=f"f-wsg-{b}")
                lu = p.add("sy", (lambda b_: lambda e: e.dma_start(
                    out=stg_sb.rearrange("p (k w c) -> p k (w c)", k=KD, w=WB),
                    in_=wu_v[:, :, b_ * WB * 128:(b_ + 1) * WB * 128]))(b),
                    waits=[("sV", cg)], inc="qWGU", amt=16, tag=f"f-lu-{tau}-{b}")
                cu = p.add("ve", lambda e: e.tensor_copy(
                    wub_sb.rearrange("p (w k c) -> p w k c", w=WB, c=128),
                    stg_sb.rearrange("p (k w c) -> p w k c", w=WB, c=128)),
                    waits=[("qWGU", lu), ("sT", p.c("sT")), ("qWS", p.c("qWS"))],
                    inc="sV", tag=f"f-cu-{tau}-{b}")
                p.add("sy", (lambda b_: lambda e: e.dma_start(
                    out=wubf_d.ap()[b_], in_=wub_sb[:, :]))(b),
                    waits=[("sV", cu)], inc="qWS", amt=16, tag=f"f-wsu-{b}")
                st["culast"] = cu
                g_rdy = [("sV", cg)]
                u_rdy = [("sV", cu)]
            else:
                lg = p.add("sy", (lambda b_: lambda e: e.dma_start(
                    out=wgb_sb[:, :], in_=wgbf_d.ap()[b_]))(b),
                    waits=[("sT", p.c("sT")), ("qWS", st["wsp_all"])],
                    inc="qWG2", amt=16, tag=f"f-lg-{tau}-{b}")
                lu = p.add("sy", (lambda b_: lambda e: e.dma_start(
                    out=wub_sb[:, :], in_=wubf_d.ap()[b_]))(b),
                    waits=[("sT", p.c("sT"))], inc="qWU2", amt=16, tag=f"f-lu-{tau}-{b}")
                g_rdy = [("qWG2", lg)]
                u_rdy = [("qWU2", lu)]
            for w in range(WB):
                m = b * WB + w
                par = m % 2
                for k in range(KD):
                    p.add("te", (lambda w_, k_, par_: lambda e: e.matmul(
                        psA[par_][:, :TAU],
                        wgb_sb[:, (w_ * KD + k_) * 128:(w_ * KD + k_ + 1) * 128],
                        xnm_sb[:, k_ * TAU:(k_ + 1) * TAU],
                        start=(k_ == 0), stop=(k_ == KD - 1)))(w, k, par),
                        waits=g_rdy + [("sV", st["t1g"][par]), ("qXM", xm)]
                        if k == 0 else [],
                        tag=f"f-g-{tau}-{m}-{k}")
                for k in range(KD):
                    p.add("te", (lambda w_, k_, par_: lambda e: e.matmul(
                        psB[par_][:, :TAU],
                        wub_sb[:, (w_ * KD + k_) * 128:(w_ * KD + k_ + 1) * 128],
                        xnm_sb[:, k_ * TAU:(k_ + 1) * TAU],
                        start=(k_ == 0), stop=(k_ == KD - 1)))(w, k, par),
                        waits=u_rdy + [("sV", st["t2g"][par])] if k == 0 else [],
                        inc="sT" if k == KD - 1 else None, tag=f"f-u-{tau}-{m}-{k}")
                gu_d = p.c("sT")
                p.add("sc", (lambda m_, par_: lambda e: e.activation(
                    fe1_sb[:, :TAU], psA[par_][:, :TAU], AF.Exp,
                    scale=ngs_sb[:, m_:m_ + 1]))(m, par),
                    waits=[("sT", gu_d), ("sV", max(st["t1last"], st["dglast"]))],
                    tag=f"f-e1-{tau}-{m}")
                p.drain("sc", f"f1-{tau}-{m}")
                p.add("sc", lambda e: e.activation(fe2_sb[:, :TAU], fe1_sb[:, :TAU],
                                                   AF.Ln, bias=1.0), tag=f"f-e2-{tau}-{m}")
                p.drain("sc", f"f2-{tau}-{m}")
                p.add("sc", lambda e: e.activation(fe1_sb[:, :TAU], fe2_sb[:, :TAU],
                                                   AF.Exp, scale=-1.0),
                      inc="sS", tag=f"f-e3-{tau}-{m}")
                sg_d = p.c("sS")
                st["t1g"][par] = p.add("ve", (lambda m_, par_: lambda e: e.scalar_tensor_tensor(
                    ft1_sb[:, par_ * 512:par_ * 512 + TAU], psA[par_][:, :TAU],
                    gs_sb[:, m_:m_ + 1], fe1_sb[:, :TAU], OP.mult, OP.mult))(m, par),
                    waits=[("sS", sg_d), ("sG", st["hdng"][par])],
                    inc="sV", tag=f"f-t1-{tau}-{m}")
                st["t1last"] = st["t1g"][par]
                st["t2g"][par] = p.add("ve", (lambda m_, par_: lambda e: e.tensor_scalar_mul(
                    ft2_sb[:, par_ * 512:par_ * 512 + TAU], psB[par_][:, :TAU],
                    us_sb[:, m_:m_ + 1]))(m, par),
                    inc="sV", tag=f"f-t2-{tau}-{m}")
                st["hdng"][par] = p.add("gp", (lambda m_, par_: lambda e: e.tensor_mul(
                    hdn_sb[:, m_ * TAU:(m_ + 1) * TAU],
                    ft1_sb[:, par_ * 512:par_ * 512 + TAU],
                    ft2_sb[:, par_ * 512:par_ * 512 + TAU]))(m, par),
                    waits=[("sV", st["t2g"][par])] +
                          ([("sT", st["dnlast"])] if m == 0 else []),
                    inc="sG", tag=f"f-hdn-{tau}-{m}")
            yield tau
        st["gulast"] = p.c("sT")
        hdn_d = p.c("sG")
        for jg in range(8):
            for bi in range(11):
                if tau == 0:
                    ldd = p.add("sy", (lambda jg_, bi_: lambda e: e.dma_start(
                        out=stgd_sb.rearrange("p (q c) -> p q c", q=4),
                        in_=wd_v[:, bi_ * 4:(bi_ + 1) * 4, jg_ * 256:(jg_ + 1) * 256]))(jg, bi),
                        waits=[("sV", st["cdlast"])], inc="qWD", amt=16,
                        tag=f"f-ld-{tau}-{jg}-{bi}")
                    cd = p.add("ve", lambda e: e.tensor_copy(wdb_sb[:, :], stgd_sb[:, :]),
                               waits=[("qWD", ldd), ("sT", p.c("sT")),
                                      ("qWS", p.c("qWS"))], inc="sV",
                               tag=f"f-cd-{tau}-{jg}-{bi}")
                    p.add("sy", (lambda jg_, bi_: lambda e: e.dma_start(
                        out=wdbf_d.ap()[jg_, bi_], in_=wdb_sb[:, :]))(jg, bi),
                        waits=[("sV", cd)], inc="qWS", amt=16, tag=f"f-wsd-{jg}-{bi}")
                    st["cdlast"] = cd
                    d_rdy = [("sV", max(cd, st["dgg"][0], st["dgg"][1]))]
                else:
                    ldd = p.add("sy", (lambda jg_, bi_: lambda e: e.dma_start(
                        out=wdb_sb[:, :], in_=wdbf_d.ap()[jg_, bi_]))(jg, bi),
                        waits=[("sT", p.c("sT")), ("qWS", st["wsp_all"])],
                        inc="qWD", amt=16, tag=f"f-ld-{tau}-{jg}-{bi}")
                    d_rdy = [("qWD", ldd),
                             ("sV", max(st["dgg"][0], st["dgg"][1]))]
                for q in range(4):
                    mi = bi * 4 + q
                    for j in range(2):
                        w0 = []
                        if q == 0 and j == 0:
                            w0 = d_rdy + [("sG", hdn_d)]
                        p.add("te", (lambda q_, j_, mi_, bi_: lambda e: e.matmul(
                            psD[j_][:, :TAU],
                            wdb_sb[:, q_ * 256 + j_ * 128:q_ * 256 + (j_ + 1) * 128],
                            hdn_sb[:, mi_ * TAU:(mi_ + 1) * TAU],
                            start=(bi_ == 0 and q_ == 0), stop=(bi_ == 10 and q_ == 3)))(
                                q, j, mi, bi),
                            waits=w0,
                            inc="sT" if (q == 3 and j == 1) else None,
                            tag=f"f-d-{tau}-{jg}-{bi}-{q}-{j}")
            dn_d = p.c("sT")
            st["dnlast"] = dn_d
            for j in range(2):
                jj = jg * 2 + j
                cl = p.add("ve", (lambda jj_, j_: lambda e: e.tensor_scalar(
                    dcl_sb[:, :TAU], psD[j_][:, :TAU], nds_sb[:, jj_:jj_ + 1], 30.0,
                    OP.mult, OP.min))(jj, j),
                    waits=[("sT", dn_d), ("sS", st["dsglast"])],
                    inc="sV", tag=f"f-dcl-{tau}-{jj}")
                p.add("sc", lambda e: e.activation(fe1_sb[:, :TAU], dcl_sb[:, :TAU], AF.Exp),
                      waits=[("sV", max(cl, st["dglast"]))], tag=f"f-de1-{tau}-{jj}")
                p.drain("sc", f"fd1-{tau}-{jj}")
                p.add("sc", lambda e: e.activation(fe2_sb[:, :TAU], fe1_sb[:, :TAU],
                                                   AF.Ln, bias=1.0), tag=f"f-de2-{tau}-{jj}")
                p.drain("sc", f"fd2-{tau}-{jj}")
                p.add("sc", lambda e: e.activation(fe1_sb[:, :TAU], fe2_sb[:, :TAU],
                                                   AF.Exp, scale=-1.0),
                      inc="sS", tag=f"f-de3-{tau}-{jj}")
                dsg_d = p.c("sS")
                st["dsglast"] = dsg_d
                st["dgg"][j] = p.add("ve", (lambda jj_, j_: lambda e: e.scalar_tensor_tensor(
                    dg_sb[:, j_ * 512:j_ * 512 + TAU], psD[j_][:, :TAU],
                    ds_sb[:, jj_:jj_ + 1], fe1_sb[:, :TAU], OP.mult, OP.mult))(jj, j),
                    waits=[("sS", dsg_d)] + ([st["gsp_pair"]] if st["gsp_pair"] else []),
                    inc="sV", tag=f"f-dg-{tau}-{jj}")
                st["dglast"] = st["dgg"][j]
            dgv = p.c("sV")
            gsem = f"qG{tau}"
            gv2 = p.add("gp", (lambda t_, jg_: lambda e: e.dma_start(
                out=g_v[:, jg_ * 2:(jg_ + 1) * 2, t_ * TAU:(t_ + 1) * TAU],
                in_=dg_sb.rearrange("p (k c) -> p k c", c=512)[:, :, :TAU]))(tau, jg),
                waits=[("sV", dgv)], inc=gsem, amt=16, tag=f"f-gsp-{tau}-{jg}")
            st["gsp_pair"] = (gsem, gv2)
            st["g_sp"][tau] = (gsem, gv2)
            yield tau
        if tau == 0:
            st["wsp_all"] = p.c("qWS")

    def emit_final(tc):
        gsem, gsp = st["g_sp"][tc]
        for jg in range(8):
            lf = 0
            loads = [
                (fg_sb, g_v[:, jg * 2:(jg + 1) * 2, tc * TAU:(tc + 1) * TAU], False),
                (fy_sb, yout_v[tc][:, jg * 2:(jg + 1) * 2, :], False),
                (fxn_sb, None, True),
                (fx_sb, xown_v[:, jg * 2:(jg + 1) * 2, tc * TAU:(tc + 1) * TAU], False),
            ]
            lw = [("qRS", tc + 1), (gsem, gsp), ("qSPA", phA_SP),
                  ("sV", st["finlast"]), ("sG", p.c("sG")), ("qOUT", p.c("qOUT"))]
            for (buf, in_ap, dyn) in loads:
                if dyn:
                    lf = p.add("sc", (lambda b_, jg_, tc_: lambda e: e.dma_start(
                        out=b_.rearrange("p (k c) -> p k c", c=512),
                        in_=xnown_v[:, jg_ * 2:(jg_ + 1) * 2,
                                    (4 + tc_) * 512:(5 + tc_) * 512]))(buf, jg, tc),
                        waits=lw, inc="qFIN", amt=16, tag=f"n-lx-{tc}-{jg}")
                else:
                    lf = p.add("sc", (lambda b_, i_: lambda e: e.dma_start(
                        out=b_.rearrange("p (k c) -> p k c", c=512), in_=i_))(buf, in_ap),
                        waits=lw, inc="qFIN", amt=16, tag=f"n-l-{tc}-{jg}")
            for j in range(2):
                kk = jg * 2 + j
                sl = slice(j * 512, (j + 1) * 512)
                p.add("ve", (lambda kk_, sl_: lambda e: e.scalar_tensor_tensor(
                    fxn_sb[:, sl_], fxn_sb[:, sl_], d_sb[:, kk_:kk_ + 1], fy_sb[:, sl_],
                    OP.mult, OP.add))(kk, sl),
                    waits=[("qFIN", lf)] if j == 0 else [],
                    inc="sV" if j == 1 else None, tag=f"n-s-{tc}-{kk}")
            s_d = p.c("sV")
            for j in range(2):
                sl = slice(j * 512, (j + 1) * 512)
                p.add("gp", (lambda sl_: lambda e: e.tensor_mul(
                    fg_sb[:, sl_], fxn_sb[:, sl_], fg_sb[:, sl_]))(sl),
                    waits=[("sV", s_d), ("qOUT", p.c("qOUT"))] if j == 0 else [],
                    inc="sG" if j == 1 else None, tag=f"n-m-{tc}-{jg}-{j}")
            m_d = p.c("sG")
            for j in range(2):
                sl = slice(j * 512, (j + 1) * 512)
                p.add("ve", (lambda sl_: lambda e: e.tensor_add(
                    fg_sb[:, sl_], fg_sb[:, sl_], fx_sb[:, sl_]))(sl),
                    waits=[("sG", m_d)] if j == 0 else [],
                    inc="sV" if j == 1 else None, tag=f"n-a-{tc}-{jg}-{j}")
            st["finlast"] = p.c("sV")
            p.add("gp", (lambda tc_, jg_: lambda e: e.dma_start(
                out=out_v[:, jg_ * 2:(jg_ + 1) * 2, tc_ * TAU:(tc_ + 1) * TAU],
                in_=fg_sb.rearrange("p (k c) -> p k c", c=512)))(tc, jg),
                waits=[("sV", st["finlast"])], inc="qOUT", amt=16, tag=f"n-o-{tc}-{jg}")

    def drive(do_ffn, do_final):
        gs = gen_scan()
        gf = gen_ffn() if do_ffn else None
        scan_done = ffn_done = False
        last_ts = -1
        rs_done = 0
        fin_done = 0
        ffn_prog = -1
        credit = 0.0
        while not (scan_done and ffn_done):
            if not scan_done:
                try:
                    last_ts = next(gs)
                except StopIteration:
                    scan_done = True
                    last_ts = NTS - 1
            while rs_done < 4 and last_ts >= 4 + rs_done and st["ysp2"][rs_done]:
                emit_rs(rs_done)
                rs_done += 1
            credit += 1.9
            while (gf is not None) and not ffn_done and credit >= 1.0:
                try:
                    ffn_prog = next(gf)
                except StopIteration:
                    ffn_done = True
                credit -= 1.0
            if gf is None:
                ffn_done = True
            if do_final:
                while (fin_done < NTAU and rs_done > fin_done
                       and st["g_sp"][fin_done]):
                    emit_final(fin_done)
                    fin_done += 1
        while rs_done < 4:
            emit_rs(rs_done)
            rs_done += 1
        if do_final:
            while fin_done < NTAU and st["g_sp"][fin_done]:
                emit_final(fin_done)
                fin_done += 1

    if stages >= 9:
        drive(True, True)
    else:
        drive(stages >= 3, stages >= 4)

    if dbg:
        _dbglist = [("dbg_xn32", [D, T], xn32_d), ("dbg_dt", [D, T], dt_d)]
        if stages >= 2:
            for q in range(4):
                _dbglist.append((f"dbg_y{q}", [D, TS], yout_q[q].ap()))
        if stages >= 3:
            _dbglist.append(("dbg_g", [D, TO], g_d))
        for nm, shp, src_ in _dbglist:
            o = nc.dram_tensor(nm, shp, F32, kind="ExternalOutput").ap()
            dbgw = [("sG", p.c("sG")), ("sV", p.c("sV")), ("sS", p.c("sS")),
                    ("qSPA", p.c("qSPA")), ("qDTS", p.c("qDTS")), ("qRS", p.c("qRS"))]
            dbgw += [(f"qY{i}", p.c(f"qY{i}")) for i in range(4)]
            dbgw += [(f"qG{i}", p.c(f"qG{i}")) for i in range(NTAU)]
            p.add("gp", (lambda o_, s_: lambda e: e.dma_start(out=o_, in_=s_))(o, src_),
                  waits=dbgw, inc="qOUT", amt=16, tag=f"dbg-{nm}")

    p.add("gp", lambda e: e.engine_nop(), waits=[("qOUT", p.c("qOUT"))], tag="out-wait")
    mn.close()

    SEMS = sorted(set(p.cnt.keys()) | {"qOUT"})
    semstack = ExitStack()
    sem_cms = {s: semstack.enter_context(nc.semaphore(s)) for s in SEMS}

    with nc.Block() as block:
        def mk(engname):
            def body(eng):
                hw = {}
                for (e, fn, waits, incs, tag) in p.ops:
                    if e != engname:
                        continue
                    for (s, v) in waits:
                        if hw.get(s, 0) >= v:
                            continue
                        eng.wait_ge(sem_cms[s], v)
                        hw[s] = v
                    inst = fn(eng)
                    for (s, amt) in incs:
                        inst.then_inc(sem_cms[s], amt)
            return body

        block.sync(mk("sy"))
        block.scalar(mk("sc"))
        block.vector(mk("ve"))
        block.tensor(mk("te"))
        block.gpsimd(mk("gp"))

    semstack.close()
    perst.close()
    return nc


def _prep_inputs(L0_attn_x, L0_attn_A_log, L0_attn_D, L0_attn_W_xproj,
                 L0_attn_W_dt, L0_attn_b_dt, L0_ffn_gate_w, L0_ffn_up_w,
                 L0_ffn_down_w, L0_ffn_gate_s, L0_ffn_up_s, L0_ffn_down_s):
    f32 = np.float32
    x = np.asarray(L0_attn_x, f32)
    wxpT = np.asarray(L0_attn_W_xproj, f32).T
    alog = np.asarray(L0_attn_A_log, f32)
    shared = {
        "wdt_t": np.ascontiguousarray(np.asarray(L0_attn_W_dt, f32).T),
        "wg_t": np.ascontiguousarray(np.asarray(L0_ffn_gate_w, f32).T),
        "wu_t": np.ascontiguousarray(np.asarray(L0_ffn_up_w, f32).T),
        "wd_t": np.ascontiguousarray(np.asarray(L0_ffn_down_w, f32).T),
        "d_r": np.ascontiguousarray(np.asarray(L0_attn_D, f32).reshape(KD, 128).T),
        "bdt_r": np.ascontiguousarray(np.asarray(L0_attn_b_dt, f32).reshape(KD, 128).T),
        "gs_r": np.ascontiguousarray(np.asarray(L0_ffn_gate_s, f32).reshape(MI, 128).T),
        "us_r": np.ascontiguousarray(np.asarray(L0_ffn_up_s, f32).reshape(MI, 128).T),
        "ds_r": np.ascontiguousarray(np.asarray(L0_ffn_down_s, f32).reshape(KD, 128).T),
        "ident": np.eye(128, dtype=f32),
        "consts_r": np.ascontiguousarray(np.tile(
            np.array([1e-5, 1.0, -0.5, -1.0, 1.0 / 2048, 30.0, 0.0, 0.0], f32),
            (128, 1))),
    }
    in_maps = []
    for c in range(8):
        b, h = c // 2, c % 2
        im = dict(shared)
        im["x_sc"] = np.ascontiguousarray(x[b])
        im["x_own"] = np.ascontiguousarray(x[b, h * TO:(h + 1) * TO].T)
        im["wxp_t"] = np.ascontiguousarray(np.concatenate(
            [wxpT[:, :R], wxpT[:, R + NH * h:R + NH * (h + 1)],
             wxpT[:, R + NST + NH * h:R + NST + NH * (h + 1)]], axis=1))
        im["alog_r"] = np.ascontiguousarray(
            alog[:, NH * h:NH * (h + 1)].reshape(KD, 128, NH)
            .transpose(1, 0, 2).reshape(128, KD * NH))
        im["off_r"] = np.array([[h * NTAU]], np.uint32)
        im["off2_r"] = np.array([[4 - h * 4]], np.uint32)
        in_maps.append(im)
    return in_maps


def kernel(**inputs):
    in_maps = _prep_inputs(**inputs)
    nc = build()
    res = run_bass_kernel_spmd(nc, in_maps, core_ids=list(range(8)))
    out = np.empty((4, T, D), np.float32)
    for c in range(8):
        b, h = c // 2, c % 2
        out[b, h * TO:(h + 1) * TO] = res.results[c]["out_ft"].T
    return out


# version 05 (final)
# speedup vs baseline: 1.2429x; 1.0954x over previous
import numpy as np
from contextlib import ExitStack
import concourse.bass as bass
import concourse.mybir as mybir
from concourse.bass_utils import run_bass_kernel_spmd

F32 = mybir.dt.float32
BF16 = mybir.dt.bfloat16
U32 = mybir.dt.uint32
AF = mybir.ActivationFunctionType
OP = mybir.AluOpType
AX = mybir.AxisListType

T, TO, D, KD, R, I, MI = 4096, 2048, 2048, 16, 64, 5632, 44
NST = 16
NH = 8
EPS = 1e-5
TS = 512
NTS = T // TS
KQ = 2
NQ = KD // KQ
NG = 2
NGR = NH // NG
TAU = 512
NTAU = TO // TAU
WB = 2
NB = MI // WB
RG = [[0, 1], [2, 3], [4, 5], [6, 7]]


class Plan:
    def __init__(self):
        self.ops = []
        self.cnt = {}
        self.pend = {}

    def c(self, sem):
        return self.cnt.get(sem, 0)

    def add(self, eng, fn, waits=(), inc=None, amt=1, tag=""):
        w = list(self.pend.pop(eng, []))
        for s, v in waits:
            assert v <= self.cnt.get(s, 0), f"wait-ahead {s}:{v}>{self.cnt.get(s,0)} @{tag}"
            if v > 0:
                w.append((s, v))
        incs = []
        if inc is not None:
            self.cnt[inc] = self.cnt.get(inc, 0) + amt
            incs.append((inc, amt))
        self.ops.append([eng, fn, tuple(w), incs, tag])
        return self.cnt.get(inc, 0) if inc is not None else None

    def drain(self, eng, tag=""):
        self.add(eng, lambda e: e.drain(), tag=f"drain-{tag}")

    def wait_next(self, eng, sem, val):
        self.pend.setdefault(eng, []).append((sem, val))


def build(dbg=False, stages=9):
    nc = bass.Bass("TRN2", target_bir_lowering=False)

    x_sc = nc.dram_tensor("x_sc", [T, D], F32, kind="ExternalInput").ap()
    x_own = nc.dram_tensor("x_own", [D, TO], F32, kind="ExternalInput").ap()
    wxp_t = nc.dram_tensor("wxp_t", [D, 80], F32, kind="ExternalInput").ap()
    wdt_t = nc.dram_tensor("wdt_t", [R, D], F32, kind="ExternalInput").ap()
    wg_t = nc.dram_tensor("wg_t", [D, I], F32, kind="ExternalInput").ap()
    wu_t = nc.dram_tensor("wu_t", [D, I], F32, kind="ExternalInput").ap()
    wd_t = nc.dram_tensor("wd_t", [I, D], F32, kind="ExternalInput").ap()
    alog_r = nc.dram_tensor("alog_r", [128, KD * NH], F32, kind="ExternalInput").ap()
    d_r = nc.dram_tensor("d_r", [128, KD], F32, kind="ExternalInput").ap()
    bdt_r = nc.dram_tensor("bdt_r", [128, KD], F32, kind="ExternalInput").ap()
    gs_r = nc.dram_tensor("gs_r", [128, MI], F32, kind="ExternalInput").ap()
    us_r = nc.dram_tensor("us_r", [128, MI], F32, kind="ExternalInput").ap()
    ds_r = nc.dram_tensor("ds_r", [128, KD], F32, kind="ExternalInput").ap()
    ident = nc.dram_tensor("ident", [128, 128], F32, kind="ExternalInput").ap()
    consts_r = nc.dram_tensor("consts_r", [128, 8], F32, kind="ExternalInput").ap()
    off_r = nc.dram_tensor("off_r", [1, 1], U32, kind="ExternalInput").ap()
    off2_r = nc.dram_tensor("off2_r", [1, 1], U32, kind="ExternalInput").ap()
    out_ft = nc.dram_tensor("out_ft", [D, TO], F32, kind="ExternalOutput").ap()

    xn32_d = nc.dram_tensor("xn32_d", [D, T], F32).ap()
    xnbf_d = nc.dram_tensor("xnbf_d", [D, T], BF16).ap()
    dt_d = nc.dram_tensor("dt_d", [D, T], F32).ap()
    xpjbf_d = nc.dram_tensor("xpjbf_d", [2 * NH, T], BF16).ap()
    yin_q = [nc.dram_tensor(f"yin{q}", [2, D, TS], F32) for q in range(4)]
    yout_q = [nc.dram_tensor(f"yout{q}", [D, TS], F32) for q in range(4)]
    g_d = nc.dram_tensor("g_d", [D, TO], F32).ap()
    xnown_d = nc.dram_tensor("xnown_d", [D, 12 * 512], F32).ap()
    wgbf_d = nc.dram_tensor("wgbf_d", [NB, 128, WB * KD * 128], BF16)
    wubf_d = nc.dram_tensor("wubf_d", [NB, 128, WB * KD * 128], BF16)
    wdbf_d = nc.dram_tensor("wdbf_d", [8, 11, 128, 4 * 256], BF16)

    p = Plan()

    rr = lambda ap_: ap_.rearrange("(k p) t -> p k t", p=128)
    xn32_v, xnbf_v, dt_v, g_v = rr(xn32_d), rr(xnbf_d), rr(dt_d), rr(g_d)
    xnown_v = rr(xnown_d)
    out_v, xown_v = rr(out_ft), rr(x_own)
    yin_v = [y.ap().rearrange("b (k p) t -> p b k t", p=128) for y in yin_q]
    yout_v = [y.ap().rearrange("(k p) t -> p k t", p=128) for y in yout_q]
    wg_v = wg_t.rearrange("(k p) i -> p k i", p=128)
    wu_v = wu_t.rearrange("(k p) i -> p k i", p=128)
    wd_v = wd_t.rearrange("(q p) h -> p q h", p=128)
    wxp_v = wxp_t.rearrange("(k p) e -> p k e", p=128)

    perst = ExitStack()
    SB = lambda n, s, d=F32: perst.enter_context(nc.sbuf_tensor(n, s, d))
    a_sb = SB("a_sb", [128, KD * NH])
    d_sb = SB("d_sb", [128, KD])
    bdt_sb = SB("bdt_sb", [128, KD])
    gs_sb = SB("gs_sb", [128, MI])
    ngs_sb = SB("ngs_sb", [128, MI])
    us_sb = SB("us_sb", [128, MI])
    ds_sb = SB("ds_sb", [128, KD])
    nds_sb = SB("nds_sb", [128, KD])
    hcar_sb = SB("hcar_sb", [128, KD * NH])
    off_sb = SB("off_sb", [1, 1], U32)
    off2_sb = SB("off2_sb", [1, 1], U32)
    cst_sb = SB("cst_sb", [128, 8])
    for _i, _cv in enumerate([EPS, 1.0, -0.5, -1.0, 1.0 / D, 30.0]):
        nc.const_aps.register(F32, _cv, cst_sb[:, _i:_i + 1])

    psA = [perst.enter_context(nc.psum_tensor(f"psA{i}", [128, 512], F32)) for i in range(3)]
    psB = [perst.enter_context(nc.psum_tensor(f"psB{i}", [128, 512], F32)) for i in range(3)]
    psD = [perst.enter_context(nc.psum_tensor(f"psD{i}", [128, 512], F32)) for i in range(2)]

    HOLD = {}

    phA = ExitStack()
    SA = lambda n, s, d=F32: phA.enter_context(nc.sbuf_tensor(n, s, d))
    idf_sb = SA("idf_sb", [128, 128])
    wxp_sb = SA("wxp_sb", [128, KD * 80], BF16)
    wdt_sb = SA("wdt_sb", [R, D], BF16)
    wxpf_sb = SA("wxpf_sb", [128, KD * 80])
    wdtf_sb = SA("wdtf_sb", [R, D])
    xtm = [SA(f"xtm{i}", [128, D]) for i in range(2)]
    x2_sb = SA("x2_sb", [128, D], BF16)
    sst = SA("sst", [128, 8])
    xntm_sb = SA("xntm_sb", [128, D])
    xnft_sb = SA("xnft_sb", [128, KD * 512])
    xnbf_sb = SA("xnbf_sb", [128, KD * 512], BF16)
    xpj_sb = SA("xpj_sb", [80, T])
    dtp_sb = SA("dtp_sb", [R, T], BF16)
    dtq_sb = SA("dtq_sb", [128, 4 * 512])
    spt_sb = SA("spt_sb", [128, 512])
    bcst_sb = SA("bcst_sb", [2 * NH, T], BF16)

    for name, dst, src in [
        ("alog", a_sb[:, :], alog_r), ("d", d_sb[:, :], d_r), ("bdt", bdt_sb[:, :], bdt_r),
        ("gs", gs_sb[:, :], gs_r), ("us", us_sb[:, :], us_r), ("ds", ds_sb[:, :], ds_r),
        ("idf", idf_sb[:, :], ident), ("csts", cst_sb[:, :], consts_r),
        ("off", off_sb[:, :], off_r), ("off2", off2_sb[:, :], off2_r),
        ("wxpf", wxpf_sb.rearrange("p (k e) -> p k e", e=80), wxp_v),
        ("wdtf", wdtf_sb[:, :], wdt_t),
    ]:
        p.add("sy", (lambda d_, s_: lambda e: e.dma_start(out=d_, in_=s_))(dst, src),
              inc="qPRE", amt=16, tag=f"pre-{name}")
    preLD = p.c("qPRE")

    p.add("sc", lambda e: e.activation(a_sb[:, :], a_sb[:, :], AF.Exp),
          waits=[("qPRE", preLD)], inc="sS", tag="pre-expA")
    p.add("ve", lambda e: e.tensor_scalar_mul(a_sb[:, :], a_sb[:, :], -1.0),
          waits=[("sS", p.c("sS"))], tag="pre-negA")
    p.add("ve", lambda e: e.tensor_scalar_mul(ngs_sb[:, :], gs_sb[:, :], -1.0),
          waits=[("qPRE", preLD)], tag="pre-neggs")
    p.add("ve", lambda e: e.tensor_scalar_mul(nds_sb[:, :], ds_sb[:, :], -1.0), tag="pre-negds")
    p.add("ve", lambda e: e.tensor_copy(wxp_sb[:, :], wxpf_sb[:, :]), tag="pre-cwxp")
    p.add("ve", lambda e: e.tensor_copy(wdt_sb[:, :], wdtf_sb[:, :]), tag="pre-cwdt")
    p.add("ve", lambda e: e.memset(hcar_sb[:, :], 0.0), inc="sV", tag="pre-hcar")
    preV = p.c("sV")

    SUM, SSQ, M_, MSQ, VAR, LV, RSTD = range(7)
    sv_ = lambda i: sst[:, i:i + 1]
    ISQD = float(1.0 / np.sqrt(np.float64(D)))

    for sc_i in range(8):
        for c4 in range(4):
            ci = sc_i * 4 + c4
            buf = xtm[ci % 2]
            xl = p.add("sy", (lambda b_, r0: lambda e: e.dma_start(
                out=b_[:, :], in_=x_sc[r0:r0 + 128, :]))(buf, ci * 128),
                waits=[("sV", p.c("sV")), ("sS", p.c("sS"))] if ci >= 2 else [],
                inc=f"qXC{ci % 2}", amt=16, tag=f"A-x-{ci}")
            p.add("ve", (lambda b_: lambda e: e.reduce_sum(sv_(SUM), b_[:, :], AX.X))(buf),
                  waits=[(f"qXC{ci % 2}", xl), ("sG", p.c("sG")), ("sS", p.c("sS"))],
                  inc="sV", tag=f"A-sum-{ci}")
            sum_d = p.c("sV")
            p.add("sc", (lambda b_: lambda e: e.activation(
                x2_sb[:, :], b_[:, :], AF.Square, scale=ISQD))(buf),
                waits=[(f"qXC{ci % 2}", xl), ("sV", p.c("sV"))], inc="sS", tag=f"A-sq-{ci}")
            sq_d = p.c("sS")
            p.add("ve", lambda e: e.reduce_sum(sv_(SSQ), x2_sb[:, :], AX.X),
                  waits=[("sS", sq_d), ("sG", p.c("sG"))], inc="sV", tag=f"A-ssq-{ci}")
            ssq_d = p.c("sV")
            p.add("sc", lambda e: e.activation(sv_(M_), sv_(SUM), AF.Copy, scale=1.0 / D),
                  waits=[("sV", sum_d)], tag=f"A-m-{ci}")
            p.add("sc", lambda e: e.activation(sv_(MSQ), sv_(SUM), AF.Square, scale=1.0 / D),
                  inc="sS", tag=f"A-msq-{ci}")
            msq_d = p.c("sS")
            p.add("gp", lambda e: e.tensor_sub(sv_(VAR), sv_(SSQ), sv_(MSQ)),
                  waits=[("sV", ssq_d), ("sS", msq_d)], inc="sG", tag=f"A-var-{ci}")
            var_d = p.c("sG")
            p.add("sc", lambda e: e.activation(sv_(LV), sv_(VAR), AF.Ln, bias=EPS),
                  waits=[("sG", var_d)], tag=f"A-lv-{ci}")
            p.drain("sc", f"Alv-{ci}")
            p.add("sc", lambda e: e.activation(sv_(RSTD), sv_(LV), AF.Exp, scale=-0.5),
                  inc="sS", tag=f"A-rstd-{ci}")
            rstd_d = p.c("sS")
            p.add("ve", (lambda b_: lambda e: e.tensor_scalar(
                xntm_sb[:, :], b_[:, :], sv_(M_), sv_(RSTD), OP.subtract, OP.mult))(buf),
                waits=[("sS", rstd_d), ("sT", p.c("sT"))], inc="sV", tag=f"A-nrm-{ci}")
            for kg in range(4):
                for j in range(4):
                    k = kg * 4 + j
                    p.add("te", (lambda k_, j_: lambda e: e.transpose(
                        psA[0][:, j_ * 128:(j_ + 1) * 128],
                        xntm_sb[:, k_ * 128:(k_ + 1) * 128], idf_sb[:, :]))(k, j),
                        waits=[("sV", p.c("sV"))] if j == 0 else [],
                        inc="sT" if j == 3 else None, tag=f"A-tr-{ci}-{k}")
                tr_d = p.c("sT")
                dst = xnft_sb.rearrange("p (k c) -> p k c", c=512)[
                    :, kg * 4:(kg + 1) * 4, c4 * 128:(c4 + 1) * 128]
                p.add("ve", (lambda d_: lambda e: e.tensor_copy(
                    d_, psA[0].rearrange("p (k c) -> p k c", c=128)))(dst),
                    waits=[("sT", tr_d)] + ([("qSPA", p.c("qSPA")), ("sS", p.c("sS")),
                                             ("qSPO", p.c("qSPO"))]
                                            if kg == 0 and c4 == 0 else []),
                    inc="sV", tag=f"A-ev-{ci}-{kg}")
        ev_all = p.c("sV")
        p.add("sc", lambda e: e.activation(xnbf_sb[:, :], xnft_sb[:, :], AF.Copy),
              waits=[("sV", ev_all), ("qSPA", p.c("qSPA")), ("sT", p.c("sT"))],
              inc="sS", tag=f"A-cast-{sc_i}")
        cast_d = p.c("sS")
        p.add("gp", (lambda s_: lambda e: e.dma_start(
            out=xn32_v[:, :, s_ * 512:(s_ + 1) * 512],
            in_=xnft_sb.rearrange("p (k c) -> p k c", c=512)))(sc_i),
            waits=[("sV", ev_all)], inc="qSPA", amt=16, tag=f"A-sp32-{sc_i}")
        p.add("gp", (lambda s_: lambda e: e.dma_start(
            out=xnbf_v[:, :, s_ * 512:(s_ + 1) * 512],
            in_=xnbf_sb.rearrange("p (k c) -> p k c", c=512)))(sc_i),
            waits=[("sS", cast_d)], inc="qSPA", amt=16, tag=f"A-spbf-{sc_i}")

        def _ownspill(s_):
            def fn(e):
                sv = e.value_load(off2_sb[0:1, 0:1])
                return e.dma_start(
                    out=xnown_v[:, :, bass.ts(sv + s_, 512)],
                    in_=xnft_sb.rearrange("p (k c) -> p k c", c=512))
            return fn
        p.add("sy", _ownspill(sc_i), waits=[("sV", ev_all), ("qPRE", preLD)],
              inc="qSPO", amt=16, tag=f"A-spown-{sc_i}")
        for k in range(KD):
            p.add("te", (lambda k_: lambda e: e.matmul(
                psA[1][0:80, :], wxp_sb[:, k_ * 80:(k_ + 1) * 80],
                xnbf_sb[:, k_ * 512:(k_ + 1) * 512],
                start=(k_ == 0), stop=(k_ == KD - 1)))(k),
                waits=[("sS", cast_d), ("sV", preV)] if k == 0 else [],
                inc="sT" if k == KD - 1 else None, tag=f"A-xp-{sc_i}-{k}")
        xp_d = p.c("sT")
        p.add("sc", (lambda s_: lambda e: e.copy(
            xpj_sb[:, s_ * 512:(s_ + 1) * 512], psA[1][0:80, :]))(sc_i),
            waits=[("sT", xp_d)], tag=f"A-xpev-{sc_i}")
        p.drain("sc", f"Axpev-{sc_i}")
        p.add("sc", (lambda s_: lambda e: e.activation(
            dtp_sb[:, s_ * 512:(s_ + 1) * 512], xpj_sb[0:R, s_ * 512:(s_ + 1) * 512],
            AF.Copy))(sc_i), inc="sS", tag=f"A-dtp-{sc_i}")
        dtp_d = p.c("sS")
        for k in range(KD):
            par = k % 2
            p.add("te", (lambda s_, k_, par_: lambda e: e.matmul(
                psD[par_][:, :], wdt_sb[:, k_ * 128:(k_ + 1) * 128],
                dtp_sb[:, s_ * 512:(s_ + 1) * 512], start=True, stop=True))(sc_i, k, par),
                waits=[("sS", dtp_d if k == 0 else p.c("sS"))],
                inc="sT", tag=f"A-dtm-{sc_i}-{k}")
            dtm_d = p.c("sT")
            p.add("sc", (lambda par_, k_: lambda e: e.activation(
                spt_sb[:, :], psD[par_][:, :], AF.Exp, bias=bdt_sb[:, k_:k_ + 1]))(par, k),
                waits=[("sT", dtm_d)] + ([("qDTS", p.c("qDTS"))] if k % 4 == 0 else []),
                tag=f"A-spe-{sc_i}-{k}")
            p.drain("sc", f"Asp-{sc_i}-{k}")
            p.add("sc", (lambda k_: lambda e: e.activation(
                dtq_sb[:, (k_ % 4) * 512:(k_ % 4 + 1) * 512], spt_sb[:, :],
                AF.Ln, bias=1.0))(k), inc="sS", tag=f"A-spl-{sc_i}-{k}")
            if k % 4 == 3:
                p.add("gp", (lambda s_, kq_: lambda e: e.dma_start(
                    out=dt_v[:, kq_ * 4:(kq_ + 1) * 4, s_ * 512:(s_ + 1) * 512],
                    in_=dtq_sb.rearrange("p (k c) -> p k c", c=512)))(sc_i, k // 4),
                    waits=[("sS", p.c("sS"))], inc="qDTS", amt=16, tag=f"A-dtsp-{sc_i}-{k}")
    p.add("sc", lambda e: e.activation(bcst_sb[:, :], xpj_sb[R:R + 2 * NH, :], AF.Copy),
          inc="sS", tag="A-bccast")
    spbc = p.add("gp", lambda e: e.dma_start(out=xpjbf_d, in_=bcst_sb[:, :]),
                 waits=[("sS", p.c("sS"))], inc="qSPA", amt=16, tag="A-spbc")
    phA_SP = p.c("qSPA")

    barw = [("sV", p.c("sV")), ("sS", p.c("sS")), ("sT", p.c("sT")), ("sG", p.c("sG")),
            ("qSPA", p.c("qSPA")), ("qDTS", p.c("qDTS")), ("qPRE", p.c("qPRE")),
            ("qSPO", p.c("qSPO")), ("qXC0", p.c("qXC0")), ("qXC1", p.c("qXC1"))]
    for eng in ("ve", "sc", "te", "gp", "sy"):
        for s, v in barw:
            p.wait_next(eng, s, v)
    phA.close()

    mn = ExitStack()
    SM = lambda n, s, d=F32: mn.enter_context(nc.sbuf_tensor(n, s, d))
    dtk_sb = SM("dtk_sb", [128, 2 * KQ * TS])
    xnk_sb = SM("xnk_sb", [128, 2 * KQ * TS])
    ab_sb = SM("ab_sb", [128, 2 * NG * TS])
    ub_sb = SM("ub_sb", [128, 2 * NG * TS])
    hb_sb = SM("hb_sb", [128, 2 * NG * TS])
    pr_sb = SM("pr_sb", [128, 2 * TS * NG])
    yk_sb = SM("yk_sb", [128, KQ * TS])
    yk2_sb = SM("yk2_sb", [128, KQ * TS])
    bbc_sb = SM("bbc_sb", [128, NGR * NG * TS], BF16)
    cbc_sb = SM("cbc_sb", [128, NGR * NG * TS], BF16)
    stg_sb = SM("stg_sb", [128, KD * WB * 128])
    wgb_sb = SM("wgb_sb", [128, WB * KD * 128], BF16)
    wub_sb = SM("wub_sb", [128, WB * KD * 128], BF16)
    stgd_sb = SM("stgd_sb", [128, 4 * 256])
    wdb_sb = SM("wdb_sb", [128, 3 * 4 * 256], BF16)
    hdn_sb = SM("hdn_sb", [128, MI * TAU], BF16)
    xnm_sb = SM("xnm_sb", [128, KD * TAU], BF16)
    fe1_sb = SM("fe1_sb", [128, 512])
    fe2_sb = SM("fe2_sb", [128, 512])
    ft1_sb = SM("ft1_sb", [128, 3 * 512])
    ft2_sb = SM("ft2_sb", [128, 3 * 512])
    dcl_sb = SM("dcl_sb", [128, 512])
    dg_sb = SM("dg_sb", [128, 2 * 512])
    fg_sb = SM("fg_sb", [128, 512])
    fy_sb = SM("fy_sb", [128, 512])
    fxn_sb = SM("fxn_sb", [128, 512])
    fx_sb = SM("fx_sb", [128, 512])

    st = dict(gcnt=0, dlv=("qDL0", 0), zlast=0, zv=0, wdi=0, wdmm=[0, 0, 0],
              ttspar=[0, 0], prodpar=[0, 0], carpar=[0, 0], redpar=[0, 0],
              ubc=[0] * NGR, pbc=[0] * NGR, bc={}, ckpar=0,
              ucp=[0, 0], zcp=[0, 0], ecp=[0, 0],
              yk_pair=None, yk2_pair=None, ysp2=[None] * 4,
              t1g=[0, 0, 0], t2g=[0, 0, 0], hdng=[0, 0, 0], dgg=[0, 0], dglast=0,
              t1last=0, culast=0, cdlast=0, gulast=0, dnlast=0, dsglast=0,
              finlast=0, gsp_pair=None, g_sp=[None] * NTAU, wsp_all=0)

    bbc_v = bbc_sb.rearrange("p (q n c) -> p q n c", q=NGR, c=TS)
    cbc_v = cbc_sb.rearrange("p (q n c) -> p q n c", q=NGR, c=TS)
    abv = ab_sb.rearrange("p (q n c) -> p q n c", q=2, c=TS)
    ubv = ub_sb.rearrange("p (q n c) -> p q n c", q=2, c=TS)
    hbv = hb_sb.rearrange("p (q n c) -> p q n c", q=2, c=TS)
    prv = pr_sb.rearrange("p (q c n) -> p q c n", q=2, n=NG)

    def emit_bcast(ts, gi):
        c0 = ts * TS
        nb = gi * NG
        p.add("sc", (lambda nb_, gi_, c0_: lambda e: e.dma_start(
            out=bbc_v[:, gi_, :, :],
            in_=xpjbf_d[nb_:nb_ + NG, c0_:c0_ + TS].unsqueeze(0).broadcast_to(
                [128, NG, TS])))(nb, gi, c0),
            waits=[("sG", st["ubc"][gi]), ("qSPA", spbc)], inc=f"qBC{gi}", amt=16,
            tag=f"s-bb-{ts}-{gi}")
        cb = p.add("sc", (lambda nb_, gi_, c0_: lambda e: e.dma_start(
            out=cbc_v[:, gi_, :, :],
            in_=xpjbf_d[NH + nb_:NH + nb_ + NG, c0_:c0_ + TS].unsqueeze(0).broadcast_to(
                [128, NG, TS])))(nb, gi, c0),
            waits=[("sG", st["pbc"][gi])], inc=f"qBC{gi}", amt=16, tag=f"s-cb-{ts}-{gi}")
        st["bc"][gi] = cb

    def gen_scan():
      for ts in range(NTS):
        c0 = ts * TS
        q4 = ts % 4
        hh = ts // 4
        for _gi in range(NGR):
            emit_bcast(ts, _gi)
        for kq in range(NQ):
            cpar = st["ckpar"]
            st["ckpar"] = 1 - cpar
            coff = cpar * KQ * TS
            xl = p.add("sy", (lambda kq_, c0_, cp_: lambda e: e.dma_start(
                out=xnk_sb.rearrange("p (q k c) -> p q k c", q=2, c=TS)[:, cp_],
                in_=xn32_v[:, kq_ * KQ:(kq_ + 1) * KQ, c0_:c0_ + TS]))(kq, c0, cpar),
                waits=[("qSPA", phA_SP), ("sG", st["ucp"][cpar])],
                inc=f"qXL{cpar}", amt=16, tag=f"s-xl-{ts}-{kq}")
            dl = p.add("sy", (lambda kq_, c0_, cp_: lambda e: e.dma_start(
                out=dtk_sb.rearrange("p (q k c) -> p q k c", q=2, c=TS)[:, cp_],
                in_=dt_v[:, kq_ * KQ:(kq_ + 1) * KQ, c0_:c0_ + TS]))(kq, c0, cpar),
                waits=[("sS", st["ecp"][cpar]), ("sG", st["zcp"][cpar])],
                inc=f"qDL{cpar}", amt=16, tag=f"s-dl-{ts}-{kq}")
            st["dlv"] = (f"qDL{cpar}", dl)
            for ki in range(KQ):
                p.add("gp", (lambda ki_, co_: lambda e: e.tensor_mul(
                    xnk_sb[:, co_ + ki_ * TS:co_ + (ki_ + 1) * TS],
                    dtk_sb[:, co_ + ki_ * TS:co_ + (ki_ + 1) * TS],
                    xnk_sb[:, co_ + ki_ * TS:co_ + (ki_ + 1) * TS]))(ki, coff),
                    waits=[(f"qDL{cpar}", dl), (f"qXL{cpar}", xl)] if ki == 0 else [],
                    inc="sG" if ki == KQ - 1 else None, tag=f"s-z-{ts}-{kq}-{ki}")
            st["zcp"][cpar] = p.c("sG")
            st["zv"] = p.c("sG")
            def stage_front(ki, gi):
                k = kq * KQ + ki
                nb = gi * NG
                cb_v = st["bc"][gi]
                par = st["gcnt"] % 2
                st["gcnt"] += 1
                for n in range(NG):
                    p.add("sc", (lambda ki_, n_, k_, nb_, par_, co_: lambda e: e.activation(
                        abv[:, par_, n_, :],
                        dtk_sb[:, co_ + ki_ * TS:co_ + (ki_ + 1) * TS], AF.Exp,
                        scale=a_sb[:, k_ * NH + nb_ + n_:k_ * NH + nb_ + n_ + 1]))(
                            ki, n, k, nb, par, coff),
                        waits=[("sV", st["ttspar"][par]), st["dlv"]]
                        if n == 0 else [],
                        inc="sS" if n == NG - 1 else None, tag=f"s-e-{ts}-{k}-{gi}-{n}")
                exp_d = p.c("sS")
                st["ecp"][cpar] = exp_d
                for n in range(NG):
                    p.add("gp", (lambda ki_, n_, par_, gi_, co_: lambda e: e.tensor_mul(
                        ubv[:, par_, n_, :],
                        xnk_sb[:, co_ + ki_ * TS:co_ + (ki_ + 1) * TS],
                        bbc_v[:, gi_, n_, :]))(ki, n, par, gi, coff),
                        waits=[(f"qBC{gi}", cb_v), ("sV", st["ttspar"][par]),
                               ("sG", st["zv"])]
                        if n == 0 else [],
                        inc="sG" if n == NG - 1 else None, tag=f"s-u-{ts}-{k}-{gi}-{n}")
                u_d = p.c("sG")
                st["ubc"][gi] = u_d
                st["ucp"][cpar] = u_d
                for n in range(NG):
                    col = k * NH + nb + n
                    p.add("ve", (lambda n_, col_, par_: lambda e: e.tensor_tensor_scan(
                        hbv[:, par_, n_, :], abv[:, par_, n_, :], ubv[:, par_, n_, :],
                        hcar_sb[:, col_:col_ + 1], OP.mult, OP.add))(n, col, par),
                        waits=[("sS", exp_d), ("sG", u_d),
                               ("sG", st["prodpar"][par]), ("sS", st["carpar"][par])]
                        if n == 0 else [],
                        inc="sV" if n == NG - 1 else None, tag=f"s-h-{ts}-{k}-{gi}-{n}")
                st["ttspar"][par] = p.c("sV")
                return (ki, gi, k, nb, par, st["ttspar"][par])

            def stage_back(job):
                ki, gi, k, nb, par, tts_d = job
                p.add("sc", (lambda k_, nb_, par_: lambda e: e.copy(
                    hcar_sb[:, k_ * NH + nb_:k_ * NH + nb_ + NG],
                    hbv[:, par_, :, TS - 1:TS]))(k, nb, par),
                    waits=[("sV", tts_d)], inc="sS", tag=f"s-c-{ts}-{k}-{gi}")
                st["carpar"][par] = p.c("sS")
                for n in range(NG):
                    p.add("gp", (lambda n_, par_, gi_: lambda e: e.tensor_mul(
                        prv[:, par_, :, n_:n_ + 1],
                        hbv[:, par_, n_, :].unsqueeze(2),
                        cbc_v[:, gi_, n_, :].unsqueeze(2)))(n, par, gi),
                        waits=[("sV", tts_d), ("sV", st["redpar"][par])]
                        if n == 0 else [],
                        inc="sG" if n == NG - 1 else None, tag=f"s-p-{ts}-{k}-{gi}-{n}")
                prod_d = p.c("sG")
                st["prodpar"][par] = prod_d
                st["pbc"][gi] = prod_d
                return prod_d

            def stage_red(job, prod_d):
                ki, gi, k, nb, par, tts_d = job
                ykb = yk_sb if gi == 1 else yk2_sb
                ypair = st["yk_pair"] if gi == 1 else st["yk2_pair"]
                pr4 = pr_sb.rearrange("p (q c n) -> p c q n", q=2, n=NG)
                p.add("ve", (lambda ki_, yb_: lambda e: e.tensor_reduce(
                    yb_[:, ki_ * TS:(ki_ + 1) * TS].unsqueeze(2).unsqueeze(3),
                    pr4, AX.XY, OP.add))(ki, ykb),
                    waits=[("sG", prod_d)] + ([ypair] if ypair and ki == 0 else []),
                    inc="sV", tag=f"s-r-{ts}-{k}-{gi}")
                st["redpar"][0] = st["redpar"][1] = p.c("sV")

            pend = None
            pend_red = None
            for ki in range(KQ):
                for gi in range(NGR):
                    job = stage_front(ki, gi)
                    if pend_red is not None:
                        stage_red(*pend_red)
                        pend_red = None
                    if pend is not None:
                        pd = stage_back(pend)
                        if pend[1] % 2 == 1:
                            pend_red = (pend, pd)
                    pend = job
            if pend_red is not None:
                stage_red(*pend_red)
                pend_red = None
            if pend is not None:
                pd = stage_back(pend)
                if pend[1] % 2 == 1:
                    stage_red(pend, pd)
            ysem = f"qY{q4}"
            allred = p.c("sV")
            y1 = p.add("gp", (lambda kq_, q_, hh_: lambda e: e.dma_start(
                out=yin_v[q_][:, hh_, kq_ * KQ:(kq_ + 1) * KQ, :],
                in_=yk_sb.rearrange("p (k c) -> p k c", c=TS)))(kq, q4, hh),
                waits=[("sV", allred)], inc=ysem, amt=16, tag=f"s-y1-{ts}-{kq}")
            st["yk_pair"] = (ysem, y1)
            y2 = p.add("gp", (lambda kq_, q_, hh_: lambda e: e.dma_start(
                out=yin_v[q_][:, hh_, kq_ * KQ:(kq_ + 1) * KQ, :],
                in_=yk2_sb.rearrange("p (k c) -> p k c", c=TS),
                accum_op=OP.add))(kq, q4, hh),
                waits=[(ysem, y1)], inc=ysem, amt=16, tag=f"s-y2-{ts}-{kq}")
            st["yk2_pair"] = (ysem, y2)
            yield ts
        if hh == 1:
            st["ysp2"][q4] = st["yk2_pair"]

    def emit_rs(q):
        ysem, yv = st["ysp2"][q]
        p.add("gp", (lambda q_: lambda e: e.collective_compute(
            "ReduceScatter", OP.add, replica_groups=RG,
            ins=[yin_q[q_].ap()], outs=[yout_q[q_].ap()]))(q),
            waits=[(ysem, yv)], inc="qRS", tag=f"rs-{q}")

    def gen_ffn():
      for tau in range(NTAU):
        def _xmfn(t_):
            def fn(e):
                sv = e.value_load(off_sb[0:1, 0:1])
                return e.dma_start(
                    out=xnm_sb.rearrange("p (k c) -> p k c", c=TAU),
                    in_=xnbf_v[:, :, bass.ts(sv + t_, TAU)])
            return fn
        xm = p.add("sy", _xmfn(tau),
                   waits=[("qSPA", phA_SP), ("sT", st["gulast"]), ("qPRE", preLD)],
                   inc="qXM", amt=16, tag=f"f-xm-{tau}")
        for b in range(NB):
            if tau == 0:
                lg = p.add("sy", (lambda b_: lambda e: e.dma_start(
                    out=stg_sb.rearrange("p (k w c) -> p k (w c)", k=KD, w=WB),
                    in_=wg_v[:, :, b_ * WB * 128:(b_ + 1) * WB * 128]))(b),
                    waits=[("sV", st["culast"])], inc="qWGU", amt=16, tag=f"f-lg-{tau}-{b}")
                cg = p.add("ve", lambda e: e.tensor_copy(
                    wgb_sb.rearrange("p (w k c) -> p w k c", w=WB, c=128),
                    stg_sb.rearrange("p (k w c) -> p w k c", w=WB, c=128)),
                    waits=[("qWGU", lg), ("sT", p.c("sT")), ("qWS", p.c("qWS"))],
                    inc="sV", tag=f"f-cg-{tau}-{b}")
                p.add("sy", (lambda b_: lambda e: e.dma_start(
                    out=wgbf_d.ap()[b_], in_=wgb_sb[:, :]))(b),
                    waits=[("sV", cg)], inc="qWS", amt=16, tag=f"f-wsg-{b}")
                lu = p.add("sy", (lambda b_: lambda e: e.dma_start(
                    out=stg_sb.rearrange("p (k w c) -> p k (w c)", k=KD, w=WB),
                    in_=wu_v[:, :, b_ * WB * 128:(b_ + 1) * WB * 128]))(b),
                    waits=[("sV", cg)], inc="qWGU", amt=16, tag=f"f-lu-{tau}-{b}")
                cu = p.add("ve", lambda e: e.tensor_copy(
                    wub_sb.rearrange("p (w k c) -> p w k c", w=WB, c=128),
                    stg_sb.rearrange("p (k w c) -> p w k c", w=WB, c=128)),
                    waits=[("qWGU", lu), ("sT", p.c("sT")), ("qWS", p.c("qWS"))],
                    inc="sV", tag=f"f-cu-{tau}-{b}")
                p.add("sy", (lambda b_: lambda e: e.dma_start(
                    out=wubf_d.ap()[b_], in_=wub_sb[:, :]))(b),
                    waits=[("sV", cu)], inc="qWS", amt=16, tag=f"f-wsu-{b}")
                st["culast"] = cu
                g_rdy = [("sV", cg)]
                u_rdy = [("sV", cu)]
            else:
                lg = p.add("sy", (lambda b_: lambda e: e.dma_start(
                    out=wgb_sb[:, :], in_=wgbf_d.ap()[b_]))(b),
                    waits=[("sT", p.c("sT")), ("qWS", st["wsp_all"])],
                    inc="qWG2", amt=16, tag=f"f-lg-{tau}-{b}")
                lu = p.add("sy", (lambda b_: lambda e: e.dma_start(
                    out=wub_sb[:, :], in_=wubf_d.ap()[b_]))(b),
                    waits=[("sT", p.c("sT"))], inc="qWU2", amt=16, tag=f"f-lu-{tau}-{b}")
                g_rdy = [("qWG2", lg)]
                u_rdy = [("qWU2", lu)]
            for w in range(WB):
                m = b * WB + w
                par = m % 3
                for k in range(KD):
                    p.add("te", (lambda w_, k_, par_: lambda e: e.matmul(
                        psA[par_][:, :TAU],
                        wgb_sb[:, (w_ * KD + k_) * 128:(w_ * KD + k_ + 1) * 128],
                        xnm_sb[:, k_ * TAU:(k_ + 1) * TAU],
                        start=(k_ == 0), stop=(k_ == KD - 1)))(w, k, par),
                        waits=g_rdy + [("sV", st["t1g"][par]), ("qXM", xm)]
                        if k == 0 else [],
                        tag=f"f-g-{tau}-{m}-{k}")
                for k in range(KD):
                    p.add("te", (lambda w_, k_, par_: lambda e: e.matmul(
                        psB[par_][:, :TAU],
                        wub_sb[:, (w_ * KD + k_) * 128:(w_ * KD + k_ + 1) * 128],
                        xnm_sb[:, k_ * TAU:(k_ + 1) * TAU],
                        start=(k_ == 0), stop=(k_ == KD - 1)))(w, k, par),
                        waits=u_rdy + [("sV", st["t2g"][par])] if k == 0 else [],
                        inc="sT" if k == KD - 1 else None, tag=f"f-u-{tau}-{m}-{k}")
                gu_d = p.c("sT")
                p.add("sc", (lambda m_, par_: lambda e: e.activation(
                    fe1_sb[:, :TAU], psA[par_][:, :TAU], AF.Exp,
                    scale=ngs_sb[:, m_:m_ + 1]))(m, par),
                    waits=[("sT", gu_d), ("sV", max(st["t1last"], st["dglast"]))],
                    tag=f"f-e1-{tau}-{m}")
                p.drain("sc", f"f1-{tau}-{m}")
                p.add("sc", lambda e: e.activation(fe2_sb[:, :TAU], fe1_sb[:, :TAU],
                                                   AF.Ln, bias=1.0), tag=f"f-e2-{tau}-{m}")
                p.drain("sc", f"f2-{tau}-{m}")
                p.add("sc", lambda e: e.activation(fe1_sb[:, :TAU], fe2_sb[:, :TAU],
                                                   AF.Exp, scale=-1.0),
                      inc="sS", tag=f"f-e3-{tau}-{m}")
                sg_d = p.c("sS")
                st["t1g"][par] = p.add("ve", (lambda m_, par_: lambda e: e.scalar_tensor_tensor(
                    ft1_sb[:, par_ * 512:par_ * 512 + TAU], psA[par_][:, :TAU],
                    gs_sb[:, m_:m_ + 1], fe1_sb[:, :TAU], OP.mult, OP.mult))(m, par),
                    waits=[("sS", sg_d), ("sG", st["hdng"][par])],
                    inc="sV", tag=f"f-t1-{tau}-{m}")
                st["t1last"] = st["t1g"][par]
                st["t2g"][par] = p.add("ve", (lambda m_, par_: lambda e: e.tensor_scalar_mul(
                    ft2_sb[:, par_ * 512:par_ * 512 + TAU], psB[par_][:, :TAU],
                    us_sb[:, m_:m_ + 1]))(m, par),
                    inc="sV", tag=f"f-t2-{tau}-{m}")
                st["hdng"][par] = p.add("gp", (lambda m_, par_: lambda e: e.tensor_mul(
                    hdn_sb[:, m_ * TAU:(m_ + 1) * TAU],
                    ft1_sb[:, par_ * 512:par_ * 512 + TAU],
                    ft2_sb[:, par_ * 512:par_ * 512 + TAU]))(m, par),
                    waits=[("sV", st["t2g"][par])] +
                          ([("sT", st["dnlast"])] if m == 0 else []),
                    inc="sG", tag=f"f-hdn-{tau}-{m}")
            yield tau
        st["gulast"] = p.c("sT")
        hdn_d = p.c("sG")
        for jg in range(8):
            for bi in range(11):
                wpar = st["wdi"] % 3
                st["wdi"] += 1
                wof = wpar * 1024
                if tau == 0:
                    ldd = p.add("sy", (lambda jg_, bi_: lambda e: e.dma_start(
                        out=stgd_sb.rearrange("p (q c) -> p q c", q=4),
                        in_=wd_v[:, bi_ * 4:(bi_ + 1) * 4, jg_ * 256:(jg_ + 1) * 256]))(jg, bi),
                        waits=[("sV", st["cdlast"])], inc="qWD", amt=16,
                        tag=f"f-ld-{tau}-{jg}-{bi}")
                    cd = p.add("ve", (lambda wo_: lambda e: e.tensor_copy(
                        wdb_sb[:, wo_:wo_ + 1024], stgd_sb[:, :]))(wof),
                               waits=[("qWD", ldd), ("sT", st["wdmm"][wpar]),
                                      ("qWS", p.c("qWS"))], inc="sV",
                               tag=f"f-cd-{tau}-{jg}-{bi}")
                    p.add("sy", (lambda jg_, bi_, wo_: lambda e: e.dma_start(
                        out=wdbf_d.ap()[jg_, bi_], in_=wdb_sb[:, wo_:wo_ + 1024]))(jg, bi, wof),
                        waits=[("sV", cd)], inc="qWS", amt=16, tag=f"f-wsd-{jg}-{bi}")
                    st["cdlast"] = cd
                    d_rdy = [("sV", max(cd, st["dgg"][0], st["dgg"][1]))]
                else:
                    ldd = p.add("sy", (lambda jg_, bi_, wo_: lambda e: e.dma_start(
                        out=wdb_sb[:, wo_:wo_ + 1024], in_=wdbf_d.ap()[jg_, bi_]))(jg, bi, wof),
                        waits=[("sT", st["wdmm"][wpar]), ("qWS", st["wsp_all"])],
                        inc=f"qWD{wpar}", amt=16, tag=f"f-ld-{tau}-{jg}-{bi}")
                    d_rdy = [(f"qWD{wpar}", ldd),
                             ("sV", max(st["dgg"][0], st["dgg"][1]))]
                for q in range(4):
                    mi = bi * 4 + q
                    for j in range(2):
                        w0 = []
                        if q == 0 and j == 0:
                            w0 = d_rdy + [("sG", hdn_d)]
                        p.add("te", (lambda q_, j_, mi_, bi_, wo_: lambda e: e.matmul(
                            psD[j_][:, :TAU],
                            wdb_sb[:, wo_ + q_ * 256 + j_ * 128:wo_ + q_ * 256 + (j_ + 1) * 128],
                            hdn_sb[:, mi_ * TAU:(mi_ + 1) * TAU],
                            start=(bi_ == 0 and q_ == 0), stop=(bi_ == 10 and q_ == 3)))(
                                q, j, mi, bi, wof),
                            waits=w0,
                            inc="sT" if (q == 3 and j == 1) else None,
                            tag=f"f-d-{tau}-{jg}-{bi}-{q}-{j}")
                st["wdmm"][wpar] = p.c("sT")
            dn_d = p.c("sT")
            st["dnlast"] = dn_d
            for j in range(2):
                jj = jg * 2 + j
                cl = p.add("ve", (lambda jj_, j_: lambda e: e.tensor_scalar(
                    dcl_sb[:, :TAU], psD[j_][:, :TAU], nds_sb[:, jj_:jj_ + 1], 30.0,
                    OP.mult, OP.min))(jj, j),
                    waits=[("sT", dn_d), ("sS", st["dsglast"])],
                    inc="sV", tag=f"f-dcl-{tau}-{jj}")
                p.add("sc", lambda e: e.activation(fe1_sb[:, :TAU], dcl_sb[:, :TAU], AF.Exp),
                      waits=[("sV", max(cl, st["dglast"]))], tag=f"f-de1-{tau}-{jj}")
                p.drain("sc", f"fd1-{tau}-{jj}")
                p.add("sc", lambda e: e.activation(fe2_sb[:, :TAU], fe1_sb[:, :TAU],
                                                   AF.Ln, bias=1.0), tag=f"f-de2-{tau}-{jj}")
                p.drain("sc", f"fd2-{tau}-{jj}")
                p.add("sc", lambda e: e.activation(fe1_sb[:, :TAU], fe2_sb[:, :TAU],
                                                   AF.Exp, scale=-1.0),
                      inc="sS", tag=f"f-de3-{tau}-{jj}")
                dsg_d = p.c("sS")
                st["dsglast"] = dsg_d
                st["dgg"][j] = p.add("ve", (lambda jj_, j_: lambda e: e.scalar_tensor_tensor(
                    dg_sb[:, j_ * 512:j_ * 512 + TAU], psD[j_][:, :TAU],
                    ds_sb[:, jj_:jj_ + 1], fe1_sb[:, :TAU], OP.mult, OP.mult))(jj, j),
                    waits=[("sS", dsg_d)] + ([st["gsp_pair"]] if st["gsp_pair"] else []),
                    inc="sV", tag=f"f-dg-{tau}-{jj}")
                st["dglast"] = st["dgg"][j]
            dgv = p.c("sV")
            gsem = f"qG{tau}"
            gv2 = p.add("gp", (lambda t_, jg_: lambda e: e.dma_start(
                out=g_v[:, jg_ * 2:(jg_ + 1) * 2, t_ * TAU:(t_ + 1) * TAU],
                in_=dg_sb.rearrange("p (k c) -> p k c", c=512)[:, :, :TAU]))(tau, jg),
                waits=[("sV", dgv)], inc=gsem, amt=16, tag=f"f-gsp-{tau}-{jg}")
            st["gsp_pair"] = (gsem, gv2)
            st["g_sp"][tau] = (gsem, gv2)
            yield tau
        if tau == 0:
            st["wsp_all"] = p.c("qWS")

    def emit_final(tc):
        gsem, gsp = st["g_sp"][tc]
        for jg in range(16):
            lf = 0
            loads = [
                (fg_sb, g_v[:, jg:jg + 1, tc * TAU:(tc + 1) * TAU], False),
                (fy_sb, yout_v[tc][:, jg:jg + 1, :], False),
                (fxn_sb, None, True),
                (fx_sb, xown_v[:, jg:jg + 1, tc * TAU:(tc + 1) * TAU], False),
            ]
            lw = [("qRS", tc + 1), (gsem, gsp), ("qSPA", phA_SP),
                  ("sV", st["finlast"]), ("sG", p.c("sG")), ("qOUT", p.c("qOUT"))]
            for (buf, in_ap, dyn) in loads:
                if dyn:
                    lf = p.add("sc", (lambda b_, jg_, tc_: lambda e: e.dma_start(
                        out=b_[:, :].unsqueeze(1),
                        in_=xnown_v[:, jg_:jg_ + 1,
                                    (4 + tc_) * 512:(5 + tc_) * 512]))(buf, jg, tc),
                        waits=lw, inc="qFIN", amt=16, tag=f"n-lx-{tc}-{jg}")
                else:
                    lf = p.add("sc", (lambda b_, i_: lambda e: e.dma_start(
                        out=b_[:, :].unsqueeze(1), in_=i_))(buf, in_ap),
                        waits=lw, inc="qFIN", amt=16, tag=f"n-l-{tc}-{jg}")
            p.add("ve", (lambda kk_: lambda e: e.scalar_tensor_tensor(
                fxn_sb[:, :], fxn_sb[:, :], d_sb[:, kk_:kk_ + 1], fy_sb[:, :],
                OP.mult, OP.add))(jg),
                waits=[("qFIN", lf)], inc="sV", tag=f"n-s-{tc}-{jg}")
            s_d = p.c("sV")
            p.add("gp", lambda e: e.tensor_mul(fg_sb[:, :], fxn_sb[:, :], fg_sb[:, :]),
                  waits=[("sV", s_d), ("qOUT", p.c("qOUT"))],
                  inc="sG", tag=f"n-m-{tc}-{jg}")
            m_d = p.c("sG")
            p.add("ve", lambda e: e.tensor_add(fg_sb[:, :], fg_sb[:, :], fx_sb[:, :]),
                  waits=[("sG", m_d)], inc="sV", tag=f"n-a-{tc}-{jg}")
            st["finlast"] = p.c("sV")
            p.add("gp", (lambda tc_, jg_: lambda e: e.dma_start(
                out=out_v[:, jg_:jg_ + 1, tc_ * TAU:(tc_ + 1) * TAU],
                in_=fg_sb[:, :].unsqueeze(1)))(tc, jg),
                waits=[("sV", st["finlast"])], inc="qOUT", amt=16, tag=f"n-o-{tc}-{jg}")

    def drive(do_ffn, do_final):
        gs = gen_scan()
        gf = gen_ffn() if do_ffn else None
        scan_done = ffn_done = False
        last_ts = -1
        rs_done = 0
        fin_done = 0
        ffn_prog = -1
        credit = 0.0
        while not (scan_done and ffn_done):
            if not scan_done:
                try:
                    last_ts = next(gs)
                except StopIteration:
                    scan_done = True
                    last_ts = NTS - 1
            while rs_done < 4 and last_ts >= 4 + rs_done and st["ysp2"][rs_done]:
                emit_rs(rs_done)
                rs_done += 1
            credit += 1.9
            while (gf is not None) and not ffn_done and credit >= 1.0:
                try:
                    ffn_prog = next(gf)
                except StopIteration:
                    ffn_done = True
                credit -= 1.0
            if gf is None:
                ffn_done = True
            if do_final:
                while (fin_done < NTAU and rs_done > fin_done
                       and st["g_sp"][fin_done]):
                    emit_final(fin_done)
                    fin_done += 1
        while rs_done < 4:
            emit_rs(rs_done)
            rs_done += 1
        if do_final:
            while fin_done < NTAU and st["g_sp"][fin_done]:
                emit_final(fin_done)
                fin_done += 1

    if stages == 30:
        for _ in gen_ffn():
            pass
    elif stages >= 9:
        drive(True, True)
    else:
        drive(stages >= 3, stages >= 4)

    if dbg:
        _dbglist = [("dbg_xn32", [D, T], xn32_d), ("dbg_dt", [D, T], dt_d)]
        if stages >= 2:
            for q in range(4):
                _dbglist.append((f"dbg_y{q}", [D, TS], yout_q[q].ap()))
        if stages >= 3:
            _dbglist.append(("dbg_g", [D, TO], g_d))
        for nm, shp, src_ in _dbglist:
            o = nc.dram_tensor(nm, shp, F32, kind="ExternalOutput").ap()
            dbgw = [("sG", p.c("sG")), ("sV", p.c("sV")), ("sS", p.c("sS")),
                    ("qSPA", p.c("qSPA")), ("qDTS", p.c("qDTS")), ("qRS", p.c("qRS"))]
            dbgw += [(f"qY{i}", p.c(f"qY{i}")) for i in range(4)]
            dbgw += [(f"qG{i}", p.c(f"qG{i}")) for i in range(NTAU)]
            p.add("gp", (lambda o_, s_: lambda e: e.dma_start(out=o_, in_=s_))(o, src_),
                  waits=dbgw, inc="qOUT", amt=16, tag=f"dbg-{nm}")

    p.add("gp", lambda e: e.engine_nop(), waits=[("qOUT", p.c("qOUT"))], tag="out-wait")
    mn.close()

    SEMS = sorted(set(p.cnt.keys()) | {"qOUT"})
    semstack = ExitStack()
    sem_cms = {s: semstack.enter_context(nc.semaphore(s)) for s in SEMS}

    with nc.Block() as block:
        def mk(engname):
            def body(eng):
                hw = {}
                for (e, fn, waits, incs, tag) in p.ops:
                    if e != engname:
                        continue
                    for (s, v) in waits:
                        if hw.get(s, 0) >= v:
                            continue
                        eng.wait_ge(sem_cms[s], v)
                        hw[s] = v
                    inst = fn(eng)
                    for (s, amt) in incs:
                        inst.then_inc(sem_cms[s], amt)
            return body

        block.sync(mk("sy"))
        block.scalar(mk("sc"))
        block.vector(mk("ve"))
        block.tensor(mk("te"))
        block.gpsimd(mk("gp"))

    semstack.close()
    perst.close()
    return nc


def _prep_inputs(L0_attn_x, L0_attn_A_log, L0_attn_D, L0_attn_W_xproj,
                 L0_attn_W_dt, L0_attn_b_dt, L0_ffn_gate_w, L0_ffn_up_w,
                 L0_ffn_down_w, L0_ffn_gate_s, L0_ffn_up_s, L0_ffn_down_s):
    f32 = np.float32
    x = np.asarray(L0_attn_x, f32)
    wxpT = np.asarray(L0_attn_W_xproj, f32).T
    alog = np.asarray(L0_attn_A_log, f32)
    shared = {
        "wdt_t": np.ascontiguousarray(np.asarray(L0_attn_W_dt, f32).T),
        "wg_t": np.ascontiguousarray(np.asarray(L0_ffn_gate_w, f32).T),
        "wu_t": np.ascontiguousarray(np.asarray(L0_ffn_up_w, f32).T),
        "wd_t": np.ascontiguousarray(np.asarray(L0_ffn_down_w, f32).T),
        "d_r": np.ascontiguousarray(np.asarray(L0_attn_D, f32).reshape(KD, 128).T),
        "bdt_r": np.ascontiguousarray(np.asarray(L0_attn_b_dt, f32).reshape(KD, 128).T),
        "gs_r": np.ascontiguousarray(np.asarray(L0_ffn_gate_s, f32).reshape(MI, 128).T),
        "us_r": np.ascontiguousarray(np.asarray(L0_ffn_up_s, f32).reshape(MI, 128).T),
        "ds_r": np.ascontiguousarray(np.asarray(L0_ffn_down_s, f32).reshape(KD, 128).T),
        "ident": np.eye(128, dtype=f32),
        "consts_r": np.ascontiguousarray(np.tile(
            np.array([1e-5, 1.0, -0.5, -1.0, 1.0 / 2048, 30.0, 0.0, 0.0], f32),
            (128, 1))),
    }
    in_maps = []
    for c in range(8):
        b, h = c // 2, c % 2
        im = dict(shared)
        im["x_sc"] = np.ascontiguousarray(x[b])
        im["x_own"] = np.ascontiguousarray(x[b, h * TO:(h + 1) * TO].T)
        im["wxp_t"] = np.ascontiguousarray(np.concatenate(
            [wxpT[:, :R], wxpT[:, R + NH * h:R + NH * (h + 1)],
             wxpT[:, R + NST + NH * h:R + NST + NH * (h + 1)]], axis=1))
        im["alog_r"] = np.ascontiguousarray(
            alog[:, NH * h:NH * (h + 1)].reshape(KD, 128, NH)
            .transpose(1, 0, 2).reshape(128, KD * NH))
        im["off_r"] = np.array([[h * NTAU]], np.uint32)
        im["off2_r"] = np.array([[4 - h * 4]], np.uint32)
        in_maps.append(im)
    return in_maps


def kernel(**inputs):
    in_maps = _prep_inputs(**inputs)
    nc = build()
    res = run_bass_kernel_spmd(nc, in_maps, core_ids=list(range(8)))
    out = np.empty((4, T, D), np.float32)
    for c in range(8):
        b, h = c // 2, c % 2
        out[b, h * TO:(h + 1) * TO] = res.results[c]["out_ft"].T
    return out
